# version 27
# baseline (speedup 1.0000x reference)
import numpy as np
import ml_dtypes
import concourse.bass as bass
import concourse.mybir as mybir
from concourse.bass_utils import run_bass_kernel_spmd
from contextlib import ExitStack

F32 = mybir.dt.float32
BF16 = mybir.dt.bfloat16
bf16 = ml_dtypes.bfloat16
AF = mybir.ActivationFunctionType
ALU = mybir.AluOpType

D = 1024
NCHUNK = 8
B_ = 4
S = 4096
T = 2048
XP = 2560
NTAU = 17
NG = 5
NFG = 4
HEADS = 16
WIN = 384


class Chan:
    def __init__(self, nc, stack, name):
        self.sem = stack.enter_context(nc.semaphore(name))
        self.n = 0

    def post(self, k=1):
        self.n += k
        return self.n

    def tok(self):
        return (self.sem, self.n)


def wtok(e, *toks):
    for s, v in toks:
        if v > 0:
            e.wait_ge(s, v)


def load_act_set(e, idx, nc):
    e.add_instruction(mybir.InstLoadActFuncSet(
        name=nc.get_next_instruction_name(),
        act_func_set_id=idx, ins=[], outs=[]))


def selfsync(e, ch, inst):
    v = ch.post()
    inst.then_inc(ch.sem, 1)
    e.wait_ge(ch.sem, v)


class Sched:
    def __init__(self):
        self.q = {"tensor": [], "vector": [], "scalar": [], "gpsimd": [], "sync": []}

    def emit(self, eng, fn):
        self.q[eng].append(fn)

    def extend(self, other):
        for k in self.q:
            self.q[k].extend(other.q[k])

    def run(self, block):
        qq = self.q
        if qq["sync"]:
            @block.sync
            def _(e):
                for fn in qq["sync"]:
                    fn(e)
        if qq["gpsimd"]:
            @block.gpsimd
            def _(e):
                for fn in qq["gpsimd"]:
                    fn(e)
        if qq["tensor"]:
            @block.tensor
            def _(e):
                for fn in qq["tensor"]:
                    fn(e)
        if qq["scalar"]:
            @block.scalar
            def _(e):
                for fn in qq["scalar"]:
                    fn(e)
        if qq["vector"]:
            @block.vector
            def _(e):
                for fn in qq["vector"]:
                    fn(e)


def build_nc(phases=3, astage=9):
    nc = bass.Bass("TRN2", target_bir_lowering=False)
    st = ExitStack()

    xpad = nc.dram_tensor("xpad", [XP, D], F32, kind="ExternalInput")
    kvmask_in = nc.dram_tensor("kvmask_h", [128, XP], F32, kind="ExternalInput")
    bmask_in = nc.dram_tensor("bmask", [128, 1], F32, kind="ExternalInput")
    wq_in = nc.dram_tensor("wq", [D, D], F32, kind="ExternalInput")
    wk_in = nc.dram_tensor("wk", [D, 128], F32, kind="ExternalInput")
    wv_in = nc.dram_tensor("wv", [D, 128], F32, kind="ExternalInput")
    wo_in = nc.dram_tensor("wo", [D, D], F32, kind="ExternalInput")
    wr_in = nc.dram_tensor("wr", [D, D], F32, kind="ExternalInput")
    wkf_in = nc.dram_tensor("wkf", [D, D], F32, kind="ExternalInput")
    wvf_in = nc.dram_tensor("wvf", [D, D], F32, kind="ExternalInput")
    bq_in = nc.dram_tensor("bq", [128, 8], F32, kind="ExternalInput")
    bk_in = nc.dram_tensor("bk", [128, 1], F32, kind="ExternalInput")
    bv_in = nc.dram_tensor("bv", [128, 1], F32, kind="ExternalInput")
    bo_in = nc.dram_tensor("bo", [1, D], F32, kind="ExternalInput")
    mur_in = nc.dram_tensor("mur", [128, 8], F32, kind="ExternalInput")
    muk_in = nc.dram_tensor("muk", [128, 8], F32, kind="ExternalInput")
    ident_in = nc.dram_tensor("ident_h", [128, 128], F32, kind="ExternalInput")
    maskq_in = nc.dram_tensor("maskq_h", [128, WIN], F32, kind="ExternalInput")
    out = nc.dram_tensor("out", [T, D], F32, kind="ExternalOutput")
    dbg = nc.dram_tensor("dbg", [128, 4096], F32, kind="ExternalOutput")

    w16q = st.enter_context(nc.sbuf_tensor("w16q", [128, 8, D], BF16))
    w16k = st.enter_context(nc.sbuf_tensor("w16k", [128, 8, 128], BF16))
    w16v = st.enter_context(nc.sbuf_tensor("w16v", [128, 8, 128], BF16))
    w16o = st.enter_context(nc.sbuf_tensor("w16o", [128, 8, D], BF16))
    w16r = st.enter_context(nc.sbuf_tensor("w16r", [128, 8, D], BF16))
    w16kf = st.enter_context(nc.sbuf_tensor("w16kf", [128, 8, D], BF16))
    w16vf = st.enter_context(nc.sbuf_tensor("w16vf", [128, 8, D], BF16))
    qy = st.enter_context(nc.sbuf_tensor("qy", [128, NTAU, 8, 128], BF16))
    k16 = st.enter_context(nc.sbuf_tensor("k16", [128, XP], BF16))
    k16b = st.enter_context(nc.sbuf_tensor("k16b", [128, XP], BF16))
    vt16 = st.enter_context(nc.sbuf_tensor("vt16", [128, 19, 128], BF16))
    yE = st.enter_context(nc.sbuf_tensor("yE", [128, D], BF16))
    ident = st.enter_context(nc.sbuf_tensor("ident", [128, 128], BF16))
    maskq = st.enter_context(nc.sbuf_tensor("maskq", [128, WIN], BF16))
    bo16 = st.enter_context(nc.sbuf_tensor("bo16", [1, D], BF16))
    ones1 = st.enter_context(nc.sbuf_tensor("ones1", [1, 128], BF16))
    bq_s = st.enter_context(nc.sbuf_tensor("bq_s", [128, 8], F32))
    bk_s = st.enter_context(nc.sbuf_tensor("bk_s", [128, 1], F32))
    bv_s = st.enter_context(nc.sbuf_tensor("bv_s", [128, 1], F32))
    mur_s = st.enter_context(nc.sbuf_tensor("mur_s", [128, 8], F32))
    muk_s = st.enter_context(nc.sbuf_tensor("muk_s", [128, 8], F32))
    bmask_s = st.enter_context(nc.sbuf_tensor("bmask_s", [128, 1], F32))
    konst = st.enter_context(nc.sbuf_tensor("konst", [128, 8], F32))

    chst = st.enter_context(ExitStack())
    sfV = Chan(nc, chst, "sfV")
    sfS = Chan(nc, chst, "sfS")
    sfG = Chan(nc, chst, "sfG")
    dW = Chan(nc, chst, "dW")
    dC = Chan(nc, chst, "dC")
    dX32 = Chan(nc, chst, "dX32")
    dOut = Chan(nc, chst, "dOut")

    with (
        nc.sbuf_tensor("x16t", [128, 3, 4, D], BF16) as x16t,
        nc.sbuf_tensor("xfm", [128, 8, 768], BF16) as xfm,
        nc.sbuf_tensor("v16", [128, XP], BF16) as v16,
        nc.sbuf_tensor("vbt", [128, 1, 1], BF16) as vbt,
        nc.sbuf_tensor("kvmask", [128, XP], BF16) as kvmask,
        nc.psum_tensor("paT", [128, 2, 512], F32) as paT,
        nc.psum_tensor("paP", [128, 2, 512], F32) as paP,
        nc.psum_tensor("paV", [128, 2, 512], F32) as paV,
        ExitStack() as ast,
    ):
        aT = Chan(nc, ast, "aT")
        aTeS = Chan(nc, ast, "aTeS")
        aTeV = Chan(nc, ast, "aTeV")
        aP = Chan(nc, ast, "aP")
        aPeS = Chan(nc, ast, "aPeS")
        aPeV = Chan(nc, ast, "aPeV")
        aVT = Chan(nc, chst, "aVT")
        aVTe = Chan(nc, ast, "aVTe")
        gd_chans = []
        atrs = [Chan(nc, ast, "aTR%d" % i) for i in range(3)]
        xfm_war = None

        sch = Sched()

        for dst, src in (
            (bq_s[:, :], bq_in.ap()),
            (bk_s[:, :], bk_in.ap()),
            (bv_s[:, :], bv_in.ap()),
            (mur_s[:, :], mur_in.ap()),
            (muk_s[:, :], muk_in.ap()),
            (bmask_s[:, :], bmask_in.ap()),
        ):
            v = dC.post(16)
            def fc(e, d=dst, s=src, v=v):
                e.wait_ge(dC.sem, v - 16)
                e.dma_start(d, s).then_inc(dC.sem, 16)
            sch.emit("sync", fc)
        n_dC = dC.n

        def g_dma(dst, src, wait_sem=None, wait_val=0):
            ch = Chan(nc, chst, "gd%d" % len(gd_chans))
            gd_chans.append(ch)
            ch.post(16)
            def f(e, d=dst, s=src, ws=wait_sem, wv=wait_val, ch=ch):
                if ws is not None and wv > 0:
                    e.wait_ge(ws, wv)
                e.dma_start(d, s).then_inc(ch.sem, 16)
            sch.emit("gpsimd", f)
            return ch.tok()

        v_ident = g_dma(ident[:, :], ident_in.ap())
        v_maskq = g_dma(maskq[:, :], maskq_in.ap())
        v_kvm = g_dma(kvmask[:, :], kvmask_in.ap())
        v_wk = g_dma(w16k[:, :, :], wk_in.ap().rearrange("(c p) n -> p c n", p=128))
        v_wv = g_dma(w16v[:, :, :], wv_in.ap().rearrange("(c p) n -> p c n", p=128))
        v_wq = g_dma(w16q[:, :, :], wq_in.ap().rearrange("(c p) n -> p c n", p=128))
        v_bo = g_dma(bo16[:, :], bo_in.ap())
        wlate = [
            ("wo", w16o, wo_in), ("wr", w16r, wr_in),
            ("wkf", w16kf, wkf_in), ("wvf", w16vf, wvf_in),
        ]
        v_wlate = {}

        sch.emit("scalar", lambda e: load_act_set(e, 6, nc))
        for col, val in ((0, -20.0), (2, 1e-5), (3, -0.5)):
            sch.emit("vector", lambda e, c=col, v=val: e.memset(konst[:, c:c + 1], v))
        sch.emit("vector", lambda e: e.memset(ones1[:, :], 1.0))

        ev_toks = []
        pe_toks = {}
        vt_done = 0
        npr = 0

        xb_tok = {}
        xb_war = {}
        def load_xbatch(b):
            if b in xb_tok or b >= 5:
                return
            rows = min(512, XP - 512 * b)
            ch = Chan(nc, chst, "gx%d" % b)
            ch.post(16)
            war = xb_war.get(b)
            def f(e, b=b, rows=rows, ch=ch, war=war):
                if war:
                    wtok(e, *war)
                e.dma_start(
                    x16t[:, b % 3, 0:rows // 128, :],
                    xpad.ap()[512 * b:512 * b + rows, :].rearrange(
                        "(a p) n -> p a n", p=128)).then_inc(ch.sem, 16)
            sch.emit("gpsimd", f)
            xb_tok[b] = ch.tok()
        for g in range(NG if astage >= 1 else 0):
            tiles = list(range(4 * g, min(4 * g + 6, 20)))
            tp_after_tile = {}
            load_xbatch(g)
            load_xbatch(g + 1)
            load_xbatch(g + 2)
            grp_tr = []
            for j, m in enumerate(tiles):
                if astage < 2:
                    continue
                slot = (m // 4) % 3
                jj = m % 4
                vdma = xb_tok[m // 4]
                col0 = 128 * j
                trc = atrs[aT.n % 3]
                vt = aT.post()
                vtr = trc.post(16)
                war = xfm_war if g >= 1 else None
                def ftx(e, slot=slot, jj=jj, col0=col0, vdma=vdma, trc=trc, vtr=vtr,
                        war=war):
                    e.wait_ge(trc.sem, vtr - 16)
                    wtok(e, vdma)
                    if war is not None:
                        wtok(e, *war)
                    e.dma_start(xfm[:, :, col0:col0 + 128],
                                x16t[:, slot, jj, :], transpose=True) \
                        .then_inc(trc.sem, 16)
                sch.emit("sync", ftx)
                grp_tr.append(trc.tok())
                tp_after_tile[j] = aT.n
            xb_war[g + 3] = tuple(grp_tr)
            ev_all = tuple(grp_tr)
            xfm_war = (aP.tok(),)

            for (name, wtile, vw, dst16, bias) in ((
                ("k", w16k, v_wk, k16, bk_s),
                ("v", w16v, v_wv, v16, bv_s),
            ) if astage >= 3 else ()):
                npr += 1
                par = npr % 2
                vP = aP.post()
                prev = npr - 2
                def fkv(e, par=par, vw=vw, wtile=wtile, ev=ev_all, vP=vP, prev=prev):
                    wtok(e, *ev)
                    wtok(e, vw)
                    if prev >= 1:
                        wtok(e, *pe_toks[prev])
                    last = None
                    for kc in range(NCHUNK):
                        last = e.matmul(paP[:, par, 0:512],
                                        wtile[:, kc, :], xfm[:, kc, 0:512],
                                        start=(kc == 0), stop=(kc == NCHUNK - 1))
                    last.then_inc(aP.sem, 1)
                sch.emit("tensor", fkv)
                aPeV.post()
                isk = (name == "k")
                def fkve(e, g=g, par=par, vP=vP, dst16=dst16, bias=bias, vkm=v_kvm,
                         isk=isk):
                    e.wait_ge(aP.sem, vP)
                    wtok(e, vkm)
                    e.wait_ge(dC.sem, n_dC)
                    if isk:
                        e.scalar_tensor_tensor(
                            dst16[:, 512 * g:512 * g + 512],
                            paP[:, par, 0:512], bias[:, 0:1],
                            kvmask[:, 512 * g:512 * g + 512],
                            ALU.add, ALU.mult).then_inc(aPeV.sem, 1)
                    else:
                        lo = 512 * g - 1
                        s0 = 1 if g == 0 else 0
                        e.scalar_tensor_tensor(
                            dst16[:, lo + s0:512 * g + 511],
                            paP[:, par, s0:512], bias[:, 0:1],
                            kvmask[:, 512 * g + s0:512 * g + 512],
                            ALU.add, ALU.mult).then_inc(aPeV.sem, 1)
                    if isk:
                        e.scalar_tensor_tensor(
                            k16b[64:128, 512 * g:512 * g + 512],
                            paP[0:64, par, 0:512], bias[0:64, 0:1],
                            kvmask[0:64, 512 * g:512 * g + 512],
                            ALU.add, ALU.mult)
                        e.scalar_tensor_tensor(
                            k16b[0:64, 512 * g:512 * g + 512],
                            paP[64:128, par, 0:512], bias[64:128, 0:1],
                            kvmask[64:128, 512 * g:512 * g + 512],
                            ALU.add, ALU.mult).then_inc(aPeV.sem, 1)
                sch.emit("vector", fkve)
                if isk:
                    aPeV.post()
                pe_toks[npr] = [aPeV.tok()]
            v_after_v = [aPeV.tok()]

            kmax = min((512 * (g + 1) - 129) // 128, 18) if astage >= 4 else -1
            if vt_done <= kmax:
                klo, khi = vt_done, kmax
                vv = aVT.post(16)
                def fvt(e, klo=klo, khi=khi, vafter=tuple(v_after_v), vv=vv):
                    e.wait_ge(aVT.sem, vv - 16)
                    wtok(e, *vafter)
                    e.dma_start(vt16[:, klo:khi + 1, :],
                                v16[:, 128 * klo:128 * (khi + 1)],
                                transpose=True).then_inc(aVT.sem, 16)
                sch.emit("sync", fvt)
                vt_done = kmax + 1

            ncols = 512 if g < 4 else 128
            nslots = ncols // 128
            for dc in range(NCHUNK if astage >= 5 else 0):
                npr += 1
                par = npr % 2
                vP = aP.post()
                prev = npr - 2
                def fq(e, dc=dc, par=par, ev=ev_all, vw=v_wq, vP=vP, prev=prev, ncols=ncols):
                    wtok(e, *ev)
                    wtok(e, vw)
                    if prev >= 1:
                        wtok(e, *pe_toks[prev])
                    last = None
                    for kc in range(NCHUNK):
                        last = e.matmul(paP[:, par, 0:ncols],
                                        w16q[:, kc, dc * 128:(dc + 1) * 128],
                                        xfm[:, kc, 129:129 + ncols],
                                        start=(kc == 0), stop=(kc == NCHUNK - 1))
                    last.then_inc(aP.sem, 1)
                sch.emit("tensor", fq)
                eng = "scalar" if (dc % 2 == 0) else "vector"
                ech = aPeS if eng == "scalar" else aPeV
                ech.post()
                def fqe(e, dc=dc, par=par, vP=vP, g=g, nslots=nslots, ncols=ncols,
                        eng=eng, ech=ech):
                    e.wait_ge(aP.sem, vP)
                    e.wait_ge(dC.sem, n_dC)
                    dst = qy[:, 4 * g:4 * g + nslots, dc, :]
                    src = paP[:, par, 0:ncols].rearrange("p (a b) -> p a b", a=nslots)
                    if eng == "scalar":
                        e.activation(dst, src, AF.Identity,
                                     bias=bq_s[:, dc:dc + 1]).then_inc(ech.sem, 1)
                    else:
                        e.tensor_scalar(dst, src, bq_s[:, dc:dc + 1],
                                        None, ALU.add).then_inc(ech.sem, 1)
                sch.emit(eng, fqe)
                pe_toks[npr] = [ech.tok()]
            xfm_war = (aP.tok(),)

        aVT_final = aVT.tok()
        wlate_defer = []
        while wlate:
            nm, wt, win_ = wlate.pop(0)
            ch = Chan(nc, chst, "gw_%s" % nm)
            ch.post(16)
            def fw(e, d=wt, s=win_, ch=ch):
                e.dma_start(d[:, :, :],
                            s.ap().rearrange("(c p) n -> p c n", p=128)) \
                    .then_inc(ch.sem, 16)
            wlate_defer.append(fw)
            v_wlate[nm] = ch.tok()
        with nc.Block() as blk:
            sch.run(blk)

    if phases < 2:
        with nc.Block() as blk:
            @blk.sync
            def _(e):
                e.dma_start(out.ap()[0:1, 0:8], bq_s[0:1, :]).then_inc(dOut.sem, 16)
                e.wait_ge(dOut.sem, 16)
        st.close()
        nc.finalize()
        return nc
    with (
        nc.sbuf_tensor("p16", [128, 2, 16, WIN], BF16) as p16,
        nc.sbuf_tensor("p16n", [128, 16, WIN], BF16) as p16n,
        nc.sbuf_tensor("pt16", [128, 8, WIN], BF16) as pt16,
        nc.sbuf_tensor("ao16", [128, 2, 8, 128], BF16) as ao16,
        nc.sbuf_tensor("x32", [128, 2, D], F32) as x32,
        nc.sbuf_tensor("z32", [128, D], F32) as z32,
        nc.sbuf_tensor("sqd", [128, D], F32) as sqd,
        nc.sbuf_tensor("sall", [128, 16], F32) as sall,
        nc.sbuf_tensor("rall", [128, 2, 16], F32) as rall,
        nc.sbuf_tensor("lnt", [128, 8], F32) as lnt,
        nc.psum_tensor("psc", [128, 3, 512], F32) as psc,
        nc.psum_tensor("po", [128, 512], F32) as po,
        nc.psum_tensor("pwo", [128, 1024], F32) as pwo,
        ExitStack() as bst,
    ):
        bQK = Chan(nc, bst, "bQK")
        trs = [Chan(nc, bst, "bTR%d" % i) for i in range(4)]
        bEX = Chan(nc, bst, "bEX")
        bRC = Chan(nc, bst, "bRC")
        bSC = Chan(nc, bst, "bSC")
        bTM = Chan(nc, bst, "bTM")
        bPE2S = Chan(nc, bst, "bPE2S")
        bPE2V = Chan(nc, bst, "bPE2V")
        bAV = Chan(nc, bst, "bAV")
        bPOE = Chan(nc, bst, "bPOE")
        bWO = Chan(nc, bst, "bWO")
        bZ = Chan(nc, bst, "bZ")
        bSQ = Chan(nc, bst, "bSQ")
        bVAR = Chan(nc, bst, "bVAR")
        bRST = Chan(nc, bst, "bRST")
        bY = Chan(nc, bst, "bY")

        sch = Sched()
        pt_toks = {}
        tr_toks = {}
        for _eng in ("tensor", "vector", "scalar", "gpsimd", "sync"):
            sch.emit(_eng, lambda e: wtok(e, v_ident, v_maskq, aVT_final))
        for fw in wlate_defer:
            sch.emit("gpsimd", fw)
        sch.emit("scalar", lambda e: load_act_set(e, 6, nc))

        prev = None
        prevprev = None
        for tau in range(NTAU):
            st_ = {"qkpe": [], "exp": [], "scale": [], "tpe": [], "evac": [],
                   "avpe": [], "po": [], "tail": Sched(), "rc": None, "fx": None}
            vx32 = dX32.post(16)
            def fx(e, tau=tau, vx32=vx32):
                e.wait_ge(dX32.sem, vx32 - 16)
                if tau >= 2:
                    e.wait_ge(bZ.sem, tau - 1)
                e.dma_start(x32[:, tau % 2, :],
                            xpad.ap()[129 + 128 * tau:257 + 128 * tau, :]) \
                    .then_inc(dX32.sem, 16)
            st_["fx"] = fx

            for h in range(HEADS):
                H = 16 * tau + h
                g = h // 8
                ch = h % 8
                hb = 64 * g
                vqk = bQK.post()
                if h % 2 == 0:
                    def fmk2(e, tau=tau, H=H):
                        for dh in range(2):
                            HH = H + dh
                            if HH >= 3:
                                e.wait_ge(bEX.sem, HH - 2)
                            e.matmul(psc[:, HH % 3, 0:WIN], ident[:, :], maskq[:, :],
                                     start=True, stop=False)
                    st_["qkpe"].append(fmk2)
                qb = 64 * (h % 2)
                ksrc = k16 if ((h % 2 == 0) == (g == 0)) else k16b
                def fqk(e, tau=tau, h=h, g=g, qb=qb, ksrc=ksrc, H=H):
                    e.matmul(psc[:, H % 3, 0:WIN],
                             qy[qb:qb + 64, tau, h // 2, :],
                             ksrc[qb:qb + 64, 1 + 128 * tau:1 + 128 * tau + WIN],
                             start=False, stop=True).then_inc(bQK.sem, 1)
                st_["qkpe"].append(fqk)
                vex = bEX.post()
                def fex(e, tau=tau, h=h, H=H, vqk=vqk):
                    e.wait_ge(bQK.sem, vqk)
                    if tau >= 2:
                        e.wait_ge(bSC.sem, 16 * (tau - 2) + h + 1)
                    if tau >= 1:
                        e.wait_ge(bRC.sem, 2 * (tau - 1) + h // 8 + 1)
                    e.activation(p16[:, tau % 2, h, :], psc[:, H % 3, 0:WIN], AF.Exp,
                                 bias=konst[:, 0:1],
                                 accum_out=sall[:, h:h + 1]).then_inc(bEX.sem, 1)
                st_["exp"].append(fex)

            rcs = []
            for half in range(2):
                need_ex = 16 * tau + 8 * (half + 1)
                vrc = bRC.post()
                def frc(e, tau=tau, half=half, need_ex=need_ex):
                    e.wait_ge(bEX.sem, need_ex)
                    if tau >= 2:
                        e.wait_ge(bSC.sem, 16 * (tau - 1))
                    e.reciprocal(rall[:, tau % 2, 8 * half:8 * half + 8],
                                 sall[:, 8 * half:8 * half + 8]).then_inc(bRC.sem, 1)
                rcs.append(frc)
            st_["rc"] = rcs

            for h in range(HEADS):
                H = 16 * tau + h
                vsc = bSC.post()
                vrc_h = 2 * tau + h // 8 + 1
                war_tr = tr_toks.get((tau - 1, h // 2))
                def fsc(e, tau=tau, h=h, H=H, vrc=vrc_h, war=war_tr):
                    e.wait_ge(bRC.sem, vrc)
                    if war is not None:
                        wtok(e, war)
                    e.tensor_scalar(p16n[:, h, :], p16[:, tau % 2, h, :],
                                    rall[:, tau % 2, h:h + 1], None, ALU.mult) \
                        .then_inc(bSC.sem, 1)
                st_["scale"].append(fsc)

            for h in range(1, HEADS, 2):
                H = 16 * tau + h
                g = h // 8
                p = h // 2
                gp = 8 * tau + p
                trc = trs[gp % 4]
                slot_a = (2 * p) % 8
                slot_b = (2 * p + 1) % 8
                war_av = 16 * tau + 2 * (p - 4) + 2 if p >= 4 else \
                    (16 * (tau - 1) + 2 * (p + 4) + 2 if tau >= 1 else 0)
                vtr = trc.post(16)
                def ftr(e, tau=tau, p=p, slot_a=slot_a, slot_b=slot_b, trc=trc,
                        vtr=vtr, war_av=war_av):
                    e.wait_ge(trc.sem, vtr - 16)
                    e.wait_ge(bSC.sem, 16 * tau + 2 * p + 2)
                    if war_av > 0:
                        e.wait_ge(bAV.sem, war_av)
                    e.dma_start(
                        pt16[:, slot_a:slot_a + 2, :].rearrange(
                            "p a (c b) -> p (a c) b", b=128),
                        p16n[:, 2 * p:2 * p + 2, :].rearrange(
                            "p a (c b) -> p (a c) b", b=128),
                        transpose=True).then_inc(trc.sem, 16)
                st_["tpe"].append(ftr)
                tok_tr = trc.tok()
                tr_toks[(tau, p)] = tok_tr
                vav = bAV.post(2)
                def fav2(e, tau=tau, h=h, g=g, H=H, p=p, tok_tr=tok_tr,
                         slot_a=slot_a, slot_b=slot_b):
                    wtok(e, tok_tr)
                    if h % 8 == 1:
                        e.wait_ge(bPOE.sem, 2 * tau + (1 if h == 9 else 0))
                    slot = (h % 8) // 2
                    last = None
                    for c in range(3):
                        lhs = vt16[:, tau + c, 64 * g:64 * g + 64]
                        last = e.matmul(
                            po[0:64, slot * 128:(slot + 1) * 128], lhs,
                            pt16[:, slot_a, c * 128:(c + 1) * 128],
                            start=(c == 0), stop=(c == 2),
                            tile_position=(0, 0), skip_group_check=True)
                        last = e.matmul(
                            po[64:128, slot * 128:(slot + 1) * 128], lhs,
                            pt16[:, slot_b, c * 128:(c + 1) * 128],
                            start=(c == 0), stop=(c == 2),
                            tile_position=(0, 64), skip_group_check=True)
                    last.then_inc(bAV.sem, 2)
                st_["avpe"].append(fav2)
                if h % 8 == 7:
                    half = h // 8
                    need_av = bAV.n
                    vpoe = bPOE.post()
                    def fpo(e, tau=tau, half=half, need_av=need_av):
                        e.wait_ge(bAV.sem, need_av)
                        if tau >= 2:
                            e.wait_ge(bWO.sem, tau - 1)
                        e.tensor_copy(ao16[:, tau % 2, 4 * half:4 * half + 4, :],
                                      po[:, 0:512].rearrange("p (a b) -> p a b", a=4)) \
                            .then_inc(bPOE.sem, 1)
                    st_["po"].append((h, fpo))
            tail = st_["tail"]
            ln_s = Sched()
            st_["ln"] = ln_s
            need_poe = bPOE.n
            vwo = bWO.post()
            def fwo(e, tau=tau, need_poe=need_poe, vbo=v_bo, vwoD=v_wlate["wo"]):
                e.wait_ge(bPOE.sem, need_poe)
                wtok(e, vbo, vwoD)
                if tau >= 1:
                    e.wait_ge(bZ.sem, tau)
                last = None
                for nh in range(2):
                    for kc in range(NCHUNK):
                        e.matmul(pwo[:, nh * 512:(nh + 1) * 512],
                                 ao16[:, tau % 2, kc, :],
                                 w16o[:, kc, nh * 512:(nh + 1) * 512],
                                 start=(kc == 0), stop=False)
                    last = e.matmul(pwo[:, nh * 512:(nh + 1) * 512],
                                    ones1[:, :], bo16[:, nh * 512:(nh + 1) * 512],
                                    start=False, stop=True)
                last.then_inc(bWO.sem, 1)
            tail.emit("tensor", fwo)
            vz = bZ.post()
            def fz(e, tau=tau, vwo=vwo, vx32=vx32):
                e.wait_ge(bWO.sem, vwo)
                e.wait_ge(dX32.sem, vx32)
                if tau >= 1:
                    e.wait_ge(bY.sem, tau)
                e.scalar_tensor_tensor(z32[:, :], pwo[:, :], 1.0,
                                       x32[:, tau % 2, :], ALU.mult, ALU.add,
                                       accum_out=lnt[:, 0:1]).then_inc(bZ.sem, 1)
            ln_s.emit("vector", fz)
            vsq = bSQ.post()
            def fsq(e, vz=vz):
                e.wait_ge(bZ.sem, vz)
                e.scalar_tensor_tensor(sqd[:, :], z32[:, :], 1.0, z32[:, :],
                                       ALU.mult, ALU.mult,
                                       accum_out=lnt[:, 1:2]).then_inc(bSQ.sem, 1)
            ln_s.emit("vector", fsq)
            vvar = bVAR.post()
            def fvar(e, vsq=vsq):
                e.wait_ge(bSQ.sem, vsq)
                selfsync(e, sfV,
                         e.tensor_scalar(lnt[:, 2:3], lnt[:, 0:1], 1.0 / 1024.0,
                                         None, ALU.mult))
                selfsync(e, sfV,
                         e.tensor_tensor(lnt[:, 3:4], lnt[:, 2:3], lnt[:, 2:3],
                                         ALU.mult))
                e.tensor_scalar(lnt[:, 4:5], lnt[:, 1:2], 1.0 / 1024.0,
                                lnt[:, 3:4], ALU.mult, ALU.subtract) \
                    .then_inc(bVAR.sem, 1)
            ln_s.emit("vector", fvar)
            vrst = bRST.post()
            def frst(e, vvar=vvar):
                e.wait_ge(bVAR.sem, vvar)
                selfsync(e, sfS,
                         e.activation(lnt[:, 5:6], lnt[:, 4:5], AF.Ln,
                                      bias=konst[:, 2:3]))
                e.activation(lnt[:, 6:7], lnt[:, 5:6], AF.Exp, scale=konst[:, 3:4]) \
                    .then_inc(bRST.sem, 1)
            ln_s.emit("scalar", frst)
            vy = bY.post()
            def fy(e, tau=tau, vrst=vrst):
                e.wait_ge(bRST.sem, vrst)
                selfsync(e, sfV,
                         e.tensor_scalar(lnt[:, 7:8], lnt[:, 2:3], lnt[:, 6:7], -1.0,
                                         ALU.mult, ALU.mult))
                if tau == 0:
                    selfsync(e, sfV,
                             e.tensor_scalar(lnt[:, 4:5], lnt[:, 6:7],
                                             bmask_s[:, 0:1], None, ALU.mult))
                    selfsync(e, sfV,
                             e.tensor_scalar(lnt[:, 5:6], lnt[:, 7:8],
                                             bmask_s[:, 0:1], None, ALU.mult))
                    e.tensor_scalar(yE[:, :], z32[:, :], lnt[:, 4:5], lnt[:, 5:6],
                                    ALU.mult, ALU.add).then_inc(bY.sem, 1)
                else:
                    dst = qy[:, tau - 1, :, :].rearrange("p a b -> p (a b)")
                    e.tensor_scalar(dst, z32[:, :], lnt[:, 6:7], lnt[:, 7:8],
                                    ALU.mult, ALU.add).then_inc(bY.sem, 1)
            ln_s.emit("vector", fy)



            sch.emit("gpsimd", st_["fx"])
            pq = st_["qkpe"]

            def emit_stage2(src_st, i):
                sch.emit("vector", src_st["scale"][2 * i])
                sch.emit("vector", src_st["scale"][2 * i + 1])
                sch.emit("sync", src_st["tpe"][i])
                sch.emit("tensor", src_st["avpe"][i])
                for hh, fp in src_st["po"]:
                    if hh == 2 * i + 1:
                        sch.emit("vector", fp)

            for i in range(8):
                if i == 3 and prevprev is not None:
                    sch.extend(prevprev["ln"])
                sch.emit("tensor", pq[3 * i])
                sch.emit("tensor", pq[3 * i + 1])
                sch.emit("tensor", pq[3 * i + 2])
                sch.emit("scalar", st_["exp"][2 * i])
                sch.emit("scalar", st_["exp"][2 * i + 1])
                if prev is not None and i >= 1:
                    emit_stage2(prev, i - 1)
                if i == 4:
                    sch.emit("vector", st_["rc"][0])
            if prev is not None:
                emit_stage2(prev, 7)
            sch.emit("vector", st_["rc"][1])
            if prev is not None:
                sch.extend(prev["tail"])
            prevprev = prev
            prev = st_

        for i in range(8):
            emit_stage2(prev, i)
            if i == 3 and prevprev is not None:
                sch.extend(prevprev["ln"])
        sch.extend(prev["tail"])
        sch.extend(prev["ln"])

        if phases == 2:
            def fdbg(e, vy=bY.n):
                e.wait_ge(bY.sem, vy)
                e.dma_start(dbg.ap()[:, 0:1024], z32[:, :]).then_inc(dOut.sem, 16)
                e.dma_start(dbg.ap()[:, 1024:2048], x32[:, (NTAU - 1) % 2, :]).then_inc(dOut.sem, 16)
                e.dma_start(dbg.ap()[:, 2048:2056], lnt[:, :]).then_inc(dOut.sem, 16)
                e.dma_start(dbg.ap()[:, 2056:2072], sall[:, :]).then_inc(dOut.sem, 16)
                e.dma_start(dbg.ap()[:, 2072:2088], rall[:, :]).then_inc(dOut.sem, 16)
                e.dma_start(dbg.ap()[:, 2088:2096], konst[:, :]).then_inc(dOut.sem, 16)
                e.dma_start(dbg.ap()[:, 2096:3120],
                            ao16[:, (NTAU - 1) % 2, :, :].rearrange("p a b -> p (a b)")) \
                    .then_inc(dOut.sem, 16)
                e.wait_ge(dOut.sem, 112)
            sch.emit("gpsimd", fdbg)
        with nc.Block() as blk:
            sch.run(blk)

    if phases < 3:
        with nc.Block() as blk:
            @blk.gpsimd
            def _(e):
                for s in range(16):
                    e.wait_ge(dOut.sem, 16 * s)
                    e.dma_start(out.ap()[128 * s:128 * (s + 1), :],
                                qy[:, s, :, :].rearrange("p a b -> p (a b)")) \
                        .then_inc(dOut.sem, 16)
                e.wait_ge(dOut.sem, 256)
        st.close()
        nc.finalize()
        return nc
    with (
        nc.sbuf_tensor("yfm", [128, 2, 8, 528], BF16) as yfm,
        nc.sbuf_tensor("d16", [128, 2, 512], BF16) as d16,
        nc.sbuf_tensor("xr16", [128, 8, 512], BF16) as xr16,
        nc.sbuf_tensor("xk16", [128, 8, 512], BF16) as xk16,
        nc.sbuf_tensor("r16", [128, 4, D], BF16) as r16,
        nc.sbuf_tensor("krelu", [128, 1, 512], BF16) as krelu,
        nc.sbuf_tensor("kact16", [128, 8, 512], BF16) as kact16,
        nc.sbuf_tensor("g32", [128, D], F32) as g32,
        nc.sbuf_tensor("z32c", [128, 2, D], F32) as z32c,
        nc.sbuf_tensor("out32", [128, D], F32) as out32,
        nc.sbuf_tensor("ybt", [128, 8, 128], BF16) as ybt,
        nc.sbuf_tensor("lnt2", [128, 8], F32) as lnt2,
        nc.psum_tensor("pr", [128, 3, 1024], F32) as pr,
        nc.psum_tensor("pk", [128, 2, 512], F32) as pk,
        ExitStack() as cst,
    ):
        cT = Chan(nc, cst, "cT")
        cTeS = Chan(nc, cst, "cTeS")
        cTeV = Chan(nc, cst, "cTeV")
        ctrs = [Chan(nc, cst, "cTR%d" % i) for i in range(2)]
        cMXV = Chan(nc, cst, "cMXV")
        cSUB = Chan(nc, cst, "cSUB")
        cMXG = Chan(nc, cst, "cMXG")
        cPRS = Chan(nc, cst, "cPRS")
        cPRV = Chan(nc, cst, "cPRV")
        cWR = Chan(nc, cst, "cWR")
        cSG = Chan(nc, cst, "cSG")
        cWK = Chan(nc, cst, "cWK")
        cRL = Chan(nc, cst, "cRL")
        cKA = Chan(nc, cst, "cKA")
        cWV = Chan(nc, cst, "cWV")
        cG = Chan(nc, cst, "cG")
        cZ2 = Chan(nc, cst, "cZ2")
        cSQ2 = Chan(nc, cst, "cSQ2")
        cVAR2 = Chan(nc, cst, "cVAR2")
        cRST2 = Chan(nc, cst, "cRST2")
        cO = Chan(nc, cst, "cO")

        sch = Sched()
        npru = 0
        cte_toks = []
        pr_toks = {}
        mx_all_tok = None
        tailsC = {}

        for G in range(NFG):
            head = Sched()
            tail = Sched()
            trcc = ctrs[G % 2]
            par = G % 2
            vtr = trcc.post(80)
            def fyt(e, G=G, par=par, trcc=trcc, vtr=vtr):
                e.wait_ge(trcc.sem, vtr - 80)
                if G >= 1:
                    e.wait_ge(cTeV.sem, G)
                if G >= 2:
                    e.wait_ge(cMXV.sem, 8 * (G - 1))
                if G == 0:
                    e.dma_start(ybt[:, :, :],
                                yE.ap().rearrange("p (a b) -> p a b", b=128),
                                transpose=True).then_inc(trcc.sem, 16)
                else:
                    e.dma_start(ybt[:, :, :], qy[:, 4 * G - 1, :, :],
                                transpose=True).then_inc(trcc.sem, 16)
                for k in range(4):
                    e.dma_start(
                        yfm[:, par, :, 16 + 128 * k:144 + 128 * k],
                        qy[:, 4 * G + k, :, :],
                        transpose=True).then_inc(trcc.sem, 16)
            head.emit("sync", fyt)
            cTeV.post()
            def fb_e(e, G=G, par=par, vtr=vtr, trcc=trcc):
                e.wait_ge(trcc.sem, vtr)
                e.tensor_copy(yfm[:, par, :, 15:16], ybt[:, :, 127:128]) \
                    .then_inc(cTeV.sem, 1)
            head.emit("vector", fb_e)
            vte_all = ((trcc.sem, vtr), cTeV.tok())
            mix_s = Sched()
            for chx in range(NCHUNK):
                vsub = cSUB.post()
                def fsub(e, G=G, par=par, chx=chx, vte=vte_all, vsub=vsub):
                    wtok(e, *vte)
                    if vsub > 2:
                        e.wait_ge(cMXV.sem, vsub - 2)
                    e.tensor_sub(d16[:, chx % 2, :], yfm[:, par, chx, 16:528],
                                 yfm[:, par, chx, 15:527]).then_inc(cSUB.sem, 1)
                mix_s.emit("gpsimd", fsub)
                vmx = cMXV.post()
                def fmx(e, G=G, par=par, chx=chx, vsub=vsub, vmx=vmx):
                    if G >= 1 and chx == 0:
                        e.wait_ge(cWR.sem, 4 * G)
                        e.wait_ge(cWK.sem, 8 * G)
                    e.wait_ge(cSUB.sem, vsub)
                    e.scalar_tensor_tensor(xr16[:, chx, :], d16[:, chx % 2, :],
                                           mur_s[:, chx:chx + 1],
                                           yfm[:, par, chx, 15:527],
                                           ALU.mult, ALU.add)
                    e.scalar_tensor_tensor(xk16[:, chx, :], d16[:, chx % 2, :],
                                           muk_s[:, chx:chx + 1],
                                           yfm[:, par, chx, 15:527],
                                           ALU.mult, ALU.add).then_inc(cMXV.sem, 1)
                mix_s.emit("vector", fmx)
            vmx_all = (cMXV.tok(),)
            mx_all_tok = vmx_all

            tail.emit("scalar", lambda e: load_act_set(e, 2, nc))
            for ts in range(4):
                npru += 1
                pru = npru
                vwr = cWR.post()
                war = pr_toks.get(pru - 3)
                def fwr(e, G=G, ts=ts, vmx=vmx_all, vwrD=v_wlate["wr"], pru=pru, war=war):
                    wtok(e, *vmx)
                    wtok(e, vwrD)
                    if war is not None:
                        wtok(e, war)
                    last = None
                    for nh in range(2):
                        for kc in range(NCHUNK):
                            last = e.matmul(pr[:, pru % 3, nh * 512:(nh + 1) * 512],
                                            xr16[:, kc, ts * 128:(ts + 1) * 128],
                                            w16r[:, kc, nh * 512:(nh + 1) * 512],
                                            start=(kc == 0), stop=(kc == NCHUNK - 1))
                    last.then_inc(cWR.sem, 1)
                tail.emit("tensor", fwr)
                vsg = cSG.post()
                pr_toks[pru] = (cSG.sem, vsg)
                def fsg(e, G=G, ts=ts, vwr=vwr, pru=pru):
                    e.wait_ge(cWR.sem, vwr)
                    if G >= 1:
                        e.wait_ge(cG.sem, 4 * G)
                    e.activation(r16[:, ts, :], pr[:, pru % 3, :], AF.Sigmoid) \
                        .then_inc(cSG.sem, 1)
                tail.emit("scalar", fsg)

            for dc in range(NCHUNK):
                vwk = cWK.post()
                def fwk(e, G=G, dc=dc, vmx=vmx_all, vwkD=v_wlate["wkf"], vwk=vwk):
                    wtok(e, *vmx)
                    wtok(e, vwkD)
                    if vwk > 2:
                        e.wait_ge(cRL.sem, vwk - 2)
                    last = None
                    for kc in range(NCHUNK):
                        last = e.matmul(pk[:, vwk % 2, 0:512],
                                        w16kf[:, kc, dc * 128:(dc + 1) * 128],
                                        xk16[:, kc, :],
                                        start=(kc == 0), stop=(kc == NCHUNK - 1))
                    last.then_inc(cWK.sem, 1)
                tail.emit("tensor", fwk)
                vrl = cRL.post()
                def frl(e, vwk=vwk, vrl=vrl):
                    e.wait_ge(cWK.sem, vwk)
                    if vrl > 1:
                        e.wait_ge(cKA.sem, vrl - 1)
                    e.activation(krelu[:, 0, :], pk[:, vwk % 2, 0:512], AF.Relu) \
                        .then_inc(cRL.sem, 1)
                tail.emit("scalar", frl)
                vka = cKA.post()
                def fka(e, G=G, dc=dc, vrl=vrl):
                    e.wait_ge(cRL.sem, vrl)
                    if G >= 1 and dc == 0:
                        e.wait_ge(cWV.sem, 4 * G)
                    e.tensor_mul(kact16[:, dc, :], krelu[:, 0, :],
                                 krelu[:, 0, :]).then_inc(cKA.sem, 1)
                tail.emit("gpsimd", fka)
            vka_all = cKA.n

            tail.emit("scalar", lambda e: load_act_set(e, 6, nc))
            for ts in range(4):
                npru += 1
                pru = npru
                vwv = cWV.post()
                war = pr_toks.get(pru - 3)
                def fwv(e, G=G, ts=ts, vka=vka_all, vwvD=v_wlate["wvf"], pru=pru, war=war):
                    e.wait_ge(cKA.sem, vka)
                    wtok(e, vwvD)
                    if war is not None:
                        wtok(e, war)
                    last = None
                    for nh in range(2):
                        for kc in range(NCHUNK):
                            last = e.matmul(pr[:, pru % 3, nh * 512:(nh + 1) * 512],
                                            kact16[:, kc, ts * 128:(ts + 1) * 128],
                                            w16vf[:, kc, nh * 512:(nh + 1) * 512],
                                            start=(kc == 0), stop=(kc == NCHUNK - 1))
                    last.then_inc(cWV.sem, 1)
                tail.emit("tensor", fwv)
                need_sg = cSG.n
                vg = cG.post()
                pr_toks[pru] = (cG.sem, vg)
                def fg(e, G=G, ts=ts, vwv=vwv, need_sg=need_sg, pru=pru):
                    e.wait_ge(cWV.sem, vwv)
                    e.wait_ge(cSG.sem, need_sg)
                    if G + ts >= 1:
                        e.wait_ge(cZ2.sem, 4 * G + ts)
                    e.tensor_mul(g32[:, :], pr[:, pru % 3, :], r16[:, ts, :]) \
                        .then_inc(cG.sem, 1)
                tail.emit("vector", fg)
                vz2 = cZ2.post()
                idx = 4 * G + ts
                def fz2(e, G=G, ts=ts, vg=vg, idx=idx):
                    e.wait_ge(cG.sem, vg)
                    if idx >= 2:
                        e.wait_ge(cO.sem, idx - 1)
                    if idx >= 1:
                        e.wait_ge(cVAR2.sem, idx)
                    e.scalar_tensor_tensor(
                        z32c[:, idx % 2, :], g32[:, :], 1.0,
                        qy[:, 4 * G + ts, :, :].rearrange("p a b -> p (a b)"),
                        ALU.mult, ALU.add,
                        accum_out=lnt2[:, 0:1]).then_inc(cZ2.sem, 1)
                tail.emit("vector", fz2)
                vsq2 = cSQ2.post()
                def fsq2(e, vz2=vz2, idx=idx):
                    e.wait_ge(cZ2.sem, vz2)
                    e.activation(g32[:, :], z32c[:, idx % 2, :], AF.Square,
                                 accum_out=lnt2[:, 1:2]).then_inc(cSQ2.sem, 1)
                tail.emit("scalar", fsq2)
                vvar2 = cVAR2.post()
                def fvar2(e, vsq2=vsq2):
                    e.wait_ge(cSQ2.sem, vsq2)
                    selfsync(e, sfV,
                             e.tensor_scalar(lnt2[:, 2:3], lnt2[:, 0:1], 1.0 / 1024.0,
                                             None, ALU.mult))
                    selfsync(e, sfV,
                             e.tensor_tensor(lnt2[:, 3:4], lnt2[:, 2:3], lnt2[:, 2:3],
                                             ALU.mult))
                    e.tensor_scalar(lnt2[:, 4:5], lnt2[:, 1:2], 1.0 / 1024.0,
                                    lnt2[:, 3:4], ALU.mult, ALU.subtract) \
                        .then_inc(cVAR2.sem, 1)
                tail.emit("vector", fvar2)
                vrst2 = cRST2.post()
                def frst2(e, vvar2=vvar2):
                    e.wait_ge(cVAR2.sem, vvar2)
                    selfsync(e, sfS,
                             e.activation(lnt2[:, 5:6], lnt2[:, 4:5], AF.Ln,
                                          bias=konst[:, 2:3]))
                    e.activation(lnt2[:, 6:7], lnt2[:, 5:6], AF.Exp,
                                 scale=konst[:, 3:4]).then_inc(cRST2.sem, 1)
                tail.emit("scalar", frst2)
                vo = cO.post()
                def fo(e, idx=idx, vrst2=vrst2):
                    e.wait_ge(cRST2.sem, vrst2)
                    if idx >= 1:
                        e.wait_ge(dOut.sem, 16 * idx)
                    selfsync(e, sfV,
                             e.tensor_scalar(lnt2[:, 7:8], lnt2[:, 2:3],
                                             lnt2[:, 6:7], -1.0,
                                             ALU.mult, ALU.mult))
                    e.tensor_scalar(out32[:, :], z32c[:, idx % 2, :],
                                    lnt2[:, 6:7], lnt2[:, 7:8],
                                    ALU.mult, ALU.add).then_inc(cO.sem, 1)
                tail.emit("vector", fo)
                vdo = dOut.post(16)
                def fdo(e, idx=idx, vo=vo, vdo=vdo):
                    e.wait_ge(dOut.sem, vdo - 16)
                    e.wait_ge(cO.sem, vo)
                    e.dma_start(out.ap()[128 * idx:128 * (idx + 1), :],
                                out32[:, :]).then_inc(dOut.sem, 16)
                tail.emit("sync", fdo)


            tailsC[G] = tail
            sch.extend(head)
            if G >= 1:
                tprev = tailsC[G - 1]
                tprev.q["vector"] = mix_s.q["vector"] + tprev.q["vector"]
                tprev.q["gpsimd"] = tprev.q["gpsimd"] + mix_s.q["gpsimd"]
                sch.extend(tprev)
            else:
                sch.extend(mix_s)
        sch.extend(tailsC[NFG - 1])

        sch.emit("sync", lambda e, v=dOut.n: e.wait_ge(dOut.sem, v))

        with nc.Block() as blk:
            sch.run(blk)

    st.close()
    nc.finalize()
    return nc


_NC_CACHE = None


def _get_nc():
    global _NC_CACHE
    if _NC_CACHE is None:
        _NC_CACHE = build_nc()
    return _NC_CACHE


def _host_inputs(L0_attn_x, L0_attn_Wq, L0_attn_bq, L0_attn_Wk, L0_attn_bk,
                 L0_attn_Wv, L0_attn_bv, L0_attn_Wo, L0_attn_bo,
                 L0_ffn_W_r, L0_ffn_W_k, L0_ffn_W_v, L0_ffn_mu_r, L0_ffn_mu_k):
    x = np.ascontiguousarray(np.asarray(L0_attn_x, dtype=np.float32))
    ident = np.eye(128, dtype=np.float32)
    r = np.arange(128)[:, None]
    w = np.arange(WIN)[None, :]
    maskq = np.where((w - r >= 0) & (w - r <= 255), 0.0, -200.0).astype(np.float32)

    shared = dict(
        wq=np.ascontiguousarray(np.asarray(L0_attn_Wq, np.float32)),
        wk=np.ascontiguousarray(np.asarray(L0_attn_Wk, np.float32)),
        wv=np.ascontiguousarray(np.asarray(L0_attn_Wv, np.float32)),
        wo=np.ascontiguousarray(np.asarray(L0_attn_Wo, np.float32)),
        wr=np.ascontiguousarray(np.asarray(L0_ffn_W_r, np.float32).T),
        wkf=np.ascontiguousarray(np.asarray(L0_ffn_W_k, np.float32).T),
        wvf=np.ascontiguousarray(np.asarray(L0_ffn_W_v, np.float32).T),
        bq=np.ascontiguousarray(np.asarray(L0_attn_bq, np.float32).reshape(8, 128).T),
        bk=np.ascontiguousarray(np.asarray(L0_attn_bk, np.float32).reshape(128, 1)),
        bv=np.ascontiguousarray(np.asarray(L0_attn_bv, np.float32).reshape(128, 1)),
        bo=np.ascontiguousarray(np.asarray(L0_attn_bo, np.float32).reshape(1, D)),
        mur=np.ascontiguousarray(np.asarray(L0_ffn_mu_r, np.float32).reshape(8, 128).T),
        muk=np.ascontiguousarray(np.asarray(L0_ffn_mu_k, np.float32).reshape(8, 128).T),
        ident_h=ident,
        maskq_h=maskq,
    )

    in_maps = []
    for core in range(8):
        b = core // 2
        h = core % 2
        t0 = h * T
        lo = t0 - 257
        xp = np.zeros((XP, D), dtype=np.float32)
        s_lo = max(lo, 0)
        s_hi = min(lo + XP, S)
        xp[s_lo - lo:s_hi - lo, :] = x[b, s_lo:s_hi, :]
        kvm = np.zeros((1, XP), dtype=np.float32)
        kvm[0, s_lo - lo:s_hi - lo] = 1.0
        kvm = np.ascontiguousarray(np.broadcast_to(kvm, (128, XP)))
        bm = np.full((128, 1), 0.0 if h == 0 else 1.0, dtype=np.float32)
        m = dict(shared)
        m["xpad"] = xp
        m["kvmask_h"] = kvm
        m["bmask"] = bm
        in_maps.append(m)
    return in_maps


def kernel(**inputs):
    in_maps = _host_inputs(**inputs)
    nc = _get_nc()
    res = run_bass_kernel_spmd(nc, in_maps, core_ids=list(range(8)))
    outs = [np.asarray(res.results[c]["out"]) for c in range(8)]
    full = np.empty((B_, S, D), dtype=np.float32)
    for core in range(8):
        b = core // 2
        h = core % 2
        full[b, h * T:(h + 1) * T, :] = outs[core]
    return full


# version 29
# speedup vs baseline: 1.0066x; 1.0066x over previous
import numpy as np
import ml_dtypes
import concourse.bass as bass
import concourse.mybir as mybir
from concourse.bass_utils import run_bass_kernel_spmd
from contextlib import ExitStack

F32 = mybir.dt.float32
BF16 = mybir.dt.bfloat16
bf16 = ml_dtypes.bfloat16
AF = mybir.ActivationFunctionType
ALU = mybir.AluOpType

D = 1024
NCHUNK = 8
B_ = 4
S = 4096
T = 2048
XP = 2560
NTAU = 17
NG = 5
NFG = 4
HEADS = 16
WIN = 384


class Chan:
    def __init__(self, nc, stack, name):
        self.sem = stack.enter_context(nc.semaphore(name))
        self.n = 0

    def post(self, k=1):
        self.n += k
        return self.n

    def tok(self):
        return (self.sem, self.n)


def wtok(e, *toks):
    for s, v in toks:
        if v > 0:
            e.wait_ge(s, v)


def load_act_set(e, idx, nc):
    e.add_instruction(mybir.InstLoadActFuncSet(
        name=nc.get_next_instruction_name(),
        act_func_set_id=idx, ins=[], outs=[]))


def selfsync(e, ch, inst):
    v = ch.post()
    inst.then_inc(ch.sem, 1)
    e.wait_ge(ch.sem, v)


class Sched:
    def __init__(self):
        self.q = {"tensor": [], "vector": [], "scalar": [], "gpsimd": [], "sync": []}

    def emit(self, eng, fn):
        self.q[eng].append(fn)

    def extend(self, other):
        for k in self.q:
            self.q[k].extend(other.q[k])

    def run(self, block):
        qq = self.q
        if qq["sync"]:
            @block.sync
            def _(e):
                for fn in qq["sync"]:
                    fn(e)
        if qq["gpsimd"]:
            @block.gpsimd
            def _(e):
                for fn in qq["gpsimd"]:
                    fn(e)
        if qq["tensor"]:
            @block.tensor
            def _(e):
                for fn in qq["tensor"]:
                    fn(e)
        if qq["scalar"]:
            @block.scalar
            def _(e):
                for fn in qq["scalar"]:
                    fn(e)
        if qq["vector"]:
            @block.vector
            def _(e):
                for fn in qq["vector"]:
                    fn(e)


def build_nc(phases=3, astage=9):
    nc = bass.Bass("TRN2", target_bir_lowering=False)
    st = ExitStack()

    xpad = nc.dram_tensor("xpad", [XP, D], F32, kind="ExternalInput")
    kvmask_in = nc.dram_tensor("kvmask_h", [128, XP], F32, kind="ExternalInput")
    bmask_in = nc.dram_tensor("bmask", [128, 1], F32, kind="ExternalInput")
    wq_in = nc.dram_tensor("wq", [D, D], F32, kind="ExternalInput")
    wk_in = nc.dram_tensor("wk", [D, 128], F32, kind="ExternalInput")
    wv_in = nc.dram_tensor("wv", [D, 128], F32, kind="ExternalInput")
    wo_in = nc.dram_tensor("wo", [D, D], F32, kind="ExternalInput")
    wr_in = nc.dram_tensor("wr", [D, D], F32, kind="ExternalInput")
    wkf_in = nc.dram_tensor("wkf", [D, D], F32, kind="ExternalInput")
    wvf_in = nc.dram_tensor("wvf", [D, D], F32, kind="ExternalInput")
    bq_in = nc.dram_tensor("bq", [128, 8], F32, kind="ExternalInput")
    bk_in = nc.dram_tensor("bk", [128, 1], F32, kind="ExternalInput")
    bv_in = nc.dram_tensor("bv", [128, 1], F32, kind="ExternalInput")
    bo_in = nc.dram_tensor("bo", [1, D], F32, kind="ExternalInput")
    mur_in = nc.dram_tensor("mur", [128, 8], F32, kind="ExternalInput")
    muk_in = nc.dram_tensor("muk", [128, 8], F32, kind="ExternalInput")
    ident_in = nc.dram_tensor("ident_h", [128, 128], F32, kind="ExternalInput")
    maskq_in = nc.dram_tensor("maskq_h", [128, WIN], F32, kind="ExternalInput")
    out = nc.dram_tensor("out", [T, D], F32, kind="ExternalOutput")
    dbg = nc.dram_tensor("dbg", [128, 4096], F32, kind="ExternalOutput")

    w16q = st.enter_context(nc.sbuf_tensor("w16q", [128, 8, D], BF16))
    w16k = st.enter_context(nc.sbuf_tensor("w16k", [128, 8, 128], BF16))
    w16v = st.enter_context(nc.sbuf_tensor("w16v", [128, 8, 128], BF16))
    w16o = st.enter_context(nc.sbuf_tensor("w16o", [128, 8, D], BF16))
    w16r = st.enter_context(nc.sbuf_tensor("w16r", [128, 8, D], BF16))
    w16kf = st.enter_context(nc.sbuf_tensor("w16kf", [128, 8, D], BF16))
    w16vf = st.enter_context(nc.sbuf_tensor("w16vf", [128, 8, D], BF16))
    qy = st.enter_context(nc.sbuf_tensor("qy", [128, NTAU, 8, 128], BF16))
    k16 = st.enter_context(nc.sbuf_tensor("k16", [128, XP], BF16))
    k16b = st.enter_context(nc.sbuf_tensor("k16b", [128, XP], BF16))
    vt16 = st.enter_context(nc.sbuf_tensor("vt16", [128, 19, 128], BF16))
    yE = st.enter_context(nc.sbuf_tensor("yE", [128, D], BF16))
    ident = st.enter_context(nc.sbuf_tensor("ident", [128, 128], BF16))
    maskq = st.enter_context(nc.sbuf_tensor("maskq", [128, WIN], BF16))
    bo16 = st.enter_context(nc.sbuf_tensor("bo16", [1, D], BF16))
    ones1 = st.enter_context(nc.sbuf_tensor("ones1", [1, 128], BF16))
    bq_s = st.enter_context(nc.sbuf_tensor("bq_s", [128, 8], F32))
    bk_s = st.enter_context(nc.sbuf_tensor("bk_s", [128, 1], F32))
    bv_s = st.enter_context(nc.sbuf_tensor("bv_s", [128, 1], F32))
    mur_s = st.enter_context(nc.sbuf_tensor("mur_s", [128, 8], F32))
    muk_s = st.enter_context(nc.sbuf_tensor("muk_s", [128, 8], F32))
    bmask_s = st.enter_context(nc.sbuf_tensor("bmask_s", [128, 1], F32))
    konst = st.enter_context(nc.sbuf_tensor("konst", [128, 8], F32))

    chst = st.enter_context(ExitStack())
    sfV = Chan(nc, chst, "sfV")
    sfS = Chan(nc, chst, "sfS")
    sfG = Chan(nc, chst, "sfG")
    dW = Chan(nc, chst, "dW")
    dC = Chan(nc, chst, "dC")
    dX32 = Chan(nc, chst, "dX32")
    dOut = Chan(nc, chst, "dOut")

    with (
        nc.sbuf_tensor("x16t", [128, 2, 4, D], BF16) as x16t,
        nc.sbuf_tensor("xfm", [128, 8, 768], BF16) as xfm,
        nc.sbuf_tensor("v16", [128, XP], BF16) as v16,
        nc.sbuf_tensor("vbt", [128, 1, 1], BF16) as vbt,
        nc.sbuf_tensor("kvmask", [128, XP], BF16) as kvmask,
        nc.psum_tensor("paT", [128, 2, 512], F32) as paT,
        nc.psum_tensor("paP", [128, 2, 512], F32) as paP,
        nc.psum_tensor("paV", [128, 2, 512], F32) as paV,
        ExitStack() as ast,
    ):
        aT = Chan(nc, ast, "aT")
        aTeS = Chan(nc, ast, "aTeS")
        aTeV = Chan(nc, ast, "aTeV")
        aP = Chan(nc, ast, "aP")
        aPeS = Chan(nc, ast, "aPeS")
        aPeV = Chan(nc, ast, "aPeV")
        aVT = Chan(nc, chst, "aVT")
        aVTe = Chan(nc, ast, "aVTe")
        gd_chans = []
        atrs = [Chan(nc, ast, "aTR%d" % i) for i in range(3)]
        xfm_war = None

        sch = Sched()

        for dst, src in (
            (bq_s[:, :], bq_in.ap()),
            (bk_s[:, :], bk_in.ap()),
            (bv_s[:, :], bv_in.ap()),
            (mur_s[:, :], mur_in.ap()),
            (muk_s[:, :], muk_in.ap()),
            (bmask_s[:, :], bmask_in.ap()),
        ):
            v = dC.post(16)
            def fc(e, d=dst, s=src, v=v):
                e.wait_ge(dC.sem, v - 16)
                e.dma_start(d, s).then_inc(dC.sem, 16)
            sch.emit("sync", fc)
        n_dC = dC.n

        def g_dma(dst, src, wait_sem=None, wait_val=0):
            ch = Chan(nc, chst, "gd%d" % len(gd_chans))
            gd_chans.append(ch)
            ch.post(16)
            def f(e, d=dst, s=src, ws=wait_sem, wv=wait_val, ch=ch):
                if ws is not None and wv > 0:
                    e.wait_ge(ws, wv)
                e.dma_start(d, s).then_inc(ch.sem, 16)
            sch.emit("gpsimd", f)
            return ch.tok()

        v_ident = g_dma(ident[:, :], ident_in.ap())
        v_maskq = g_dma(maskq[:, :], maskq_in.ap())
        v_kvm = g_dma(kvmask[:, :], kvmask_in.ap())
        v_wk = g_dma(w16k[:, :, :], wk_in.ap().rearrange("(c p) n -> p c n", p=128))
        v_wv = g_dma(w16v[:, :, :], wv_in.ap().rearrange("(c p) n -> p c n", p=128))
        wlate = [
            ("wo", w16o, wo_in), ("wr", w16r, wr_in),
            ("wkf", w16kf, wkf_in), ("wvf", w16vf, wvf_in),
        ]
        v_wlate = {}

        sch.emit("scalar", lambda e: load_act_set(e, 6, nc))
        for col, val in ((0, -20.0), (2, 1e-5), (3, -0.5)):
            sch.emit("vector", lambda e, c=col, v=val: e.memset(konst[:, c:c + 1], v))
        sch.emit("vector", lambda e: e.memset(ones1[:, :], 1.0))

        ev_toks = []
        pe_toks = {}
        vt_done = 0
        npr = 0

        xb_tok = {}
        xb_war = {}
        def load_xbatch(b):
            if b in xb_tok or b >= 5:
                return
            rows = min(512, XP - 512 * b)
            ch = Chan(nc, chst, "gx%d" % b)
            ch.post(16)
            war = xb_war.get(b)
            def f(e, b=b, rows=rows, ch=ch, war=war):
                if war:
                    wtok(e, *war)
                e.dma_start(
                    x16t[:, b % 2, 0:rows // 128, :],
                    xpad.ap()[512 * b:512 * b + rows, :].rearrange(
                        "(a p) n -> p a n", p=128)).then_inc(ch.sem, 16)
            sch.emit("gpsimd", f)
            xb_tok[b] = ch.tok()
        load_xbatch(0)
        v_wq = g_dma(w16q[:, :, :], wq_in.ap().rearrange("(c p) n -> p c n", p=128))
        v_bo = g_dma(bo16[:, :], bo_in.ap())
        for g in range(NG if astage >= 1 else 0):
            tiles = list(range(4 * g, min(4 * g + 6, 20)))
            tp_after_tile = {}
            load_xbatch(g)
            load_xbatch(g + 1)
            grp_tr = []
            for j, m in enumerate(tiles):
                if astage < 2:
                    continue
                slot = (m // 4) % 2
                jj = m % 4
                vdma = xb_tok[m // 4]
                col0 = 128 * j
                trc = atrs[aT.n % 3]
                vt = aT.post()
                vtr = trc.post(16)
                war = xfm_war if g >= 1 else None
                def ftx(e, slot=slot, jj=jj, col0=col0, vdma=vdma, trc=trc, vtr=vtr,
                        war=war):
                    e.wait_ge(trc.sem, vtr - 16)
                    wtok(e, vdma)
                    if war is not None:
                        wtok(e, *war)
                    e.dma_start(xfm[:, :, col0:col0 + 128],
                                x16t[:, slot, jj, :], transpose=True) \
                        .then_inc(trc.sem, 16)
                sch.emit("sync", ftx)
                grp_tr.append(trc.tok())
                tp_after_tile[j] = aT.n
            xb_war[g + 2] = tuple(grp_tr)
            ev_all = tuple(grp_tr)
            xfm_war = (aP.tok(),)

            for (name, wtile, vw, dst16, bias) in ((
                ("k", w16k, v_wk, k16, bk_s),
                ("v", w16v, v_wv, v16, bv_s),
            ) if astage >= 3 else ()):
                npr += 1
                par = npr % 2
                vP = aP.post()
                prev = npr - 2
                def fkv(e, par=par, vw=vw, wtile=wtile, ev=ev_all, vP=vP, prev=prev):
                    wtok(e, *ev)
                    wtok(e, vw)
                    if prev >= 1:
                        wtok(e, *pe_toks[prev])
                    last = None
                    for kc in range(NCHUNK):
                        last = e.matmul(paP[:, par, 0:512],
                                        wtile[:, kc, :], xfm[:, kc, 0:512],
                                        start=(kc == 0), stop=(kc == NCHUNK - 1))
                    last.then_inc(aP.sem, 1)
                sch.emit("tensor", fkv)
                aPeV.post()
                isk = (name == "k")
                def fkve(e, g=g, par=par, vP=vP, dst16=dst16, bias=bias, vkm=v_kvm,
                         isk=isk):
                    e.wait_ge(aP.sem, vP)
                    wtok(e, vkm)
                    e.wait_ge(dC.sem, n_dC)
                    if isk:
                        e.scalar_tensor_tensor(
                            dst16[:, 512 * g:512 * g + 512],
                            paP[:, par, 0:512], bias[:, 0:1],
                            kvmask[:, 512 * g:512 * g + 512],
                            ALU.add, ALU.mult).then_inc(aPeV.sem, 1)
                    else:
                        lo = 512 * g - 1
                        s0 = 1 if g == 0 else 0
                        e.scalar_tensor_tensor(
                            dst16[:, lo + s0:512 * g + 511],
                            paP[:, par, s0:512], bias[:, 0:1],
                            kvmask[:, 512 * g + s0:512 * g + 512],
                            ALU.add, ALU.mult).then_inc(aPeV.sem, 1)
                    if isk:
                        e.scalar_tensor_tensor(
                            k16b[64:128, 512 * g:512 * g + 512],
                            paP[0:64, par, 0:512], bias[0:64, 0:1],
                            kvmask[0:64, 512 * g:512 * g + 512],
                            ALU.add, ALU.mult)
                        e.scalar_tensor_tensor(
                            k16b[0:64, 512 * g:512 * g + 512],
                            paP[64:128, par, 0:512], bias[64:128, 0:1],
                            kvmask[64:128, 512 * g:512 * g + 512],
                            ALU.add, ALU.mult).then_inc(aPeV.sem, 1)
                sch.emit("vector", fkve)
                if isk:
                    aPeV.post()
                pe_toks[npr] = [aPeV.tok()]
            v_after_v = [aPeV.tok()]

            kmax = min((512 * (g + 1) - 129) // 128, 18) if astage >= 4 else -1
            if vt_done <= kmax:
                klo, khi = vt_done, kmax
                vv = aVT.post(16)
                def fvt(e, klo=klo, khi=khi, vafter=tuple(v_after_v), vv=vv):
                    e.wait_ge(aVT.sem, vv - 16)
                    wtok(e, *vafter)
                    e.dma_start(vt16[:, klo:khi + 1, :],
                                v16[:, 128 * klo:128 * (khi + 1)],
                                transpose=True).then_inc(aVT.sem, 16)
                sch.emit("sync", fvt)
                vt_done = kmax + 1

            ncols = 512 if g < 4 else 128
            nslots = ncols // 128
            for dc in range(NCHUNK if astage >= 5 else 0):
                npr += 1
                par = npr % 2
                vP = aP.post()
                prev = npr - 2
                def fq(e, dc=dc, par=par, ev=ev_all, vw=v_wq, vP=vP, prev=prev, ncols=ncols):
                    wtok(e, *ev)
                    wtok(e, vw)
                    if prev >= 1:
                        wtok(e, *pe_toks[prev])
                    last = None
                    for kc in range(NCHUNK):
                        last = e.matmul(paP[:, par, 0:ncols],
                                        w16q[:, kc, dc * 128:(dc + 1) * 128],
                                        xfm[:, kc, 129:129 + ncols],
                                        start=(kc == 0), stop=(kc == NCHUNK - 1))
                    last.then_inc(aP.sem, 1)
                sch.emit("tensor", fq)
                eng = "scalar" if (dc % 2 == 0) else "vector"
                ech = aPeS if eng == "scalar" else aPeV
                ech.post()
                def fqe(e, dc=dc, par=par, vP=vP, g=g, nslots=nslots, ncols=ncols,
                        eng=eng, ech=ech):
                    e.wait_ge(aP.sem, vP)
                    e.wait_ge(dC.sem, n_dC)
                    dst = qy[:, 4 * g:4 * g + nslots, dc, :]
                    src = paP[:, par, 0:ncols].rearrange("p (a b) -> p a b", a=nslots)
                    if eng == "scalar":
                        e.activation(dst, src, AF.Identity,
                                     bias=bq_s[:, dc:dc + 1]).then_inc(ech.sem, 1)
                    else:
                        e.tensor_scalar(dst, src, bq_s[:, dc:dc + 1],
                                        None, ALU.add).then_inc(ech.sem, 1)
                sch.emit(eng, fqe)
                pe_toks[npr] = [ech.tok()]
            xfm_war = (aP.tok(),)

        aVT_final = aVT.tok()
        wlate_defer = []
        while wlate:
            nm, wt, win_ = wlate.pop(0)
            ch = Chan(nc, chst, "gw_%s" % nm)
            ch.post(16)
            def fw(e, d=wt, s=win_, ch=ch):
                e.dma_start(d[:, :, :],
                            s.ap().rearrange("(c p) n -> p c n", p=128)) \
                    .then_inc(ch.sem, 16)
            wlate_defer.append(fw)
            v_wlate[nm] = ch.tok()
        with nc.Block() as blk:
            sch.run(blk)

    if phases < 2:
        with nc.Block() as blk:
            @blk.sync
            def _(e):
                e.dma_start(out.ap()[0:1, 0:8], bq_s[0:1, :]).then_inc(dOut.sem, 16)
                e.wait_ge(dOut.sem, 16)
        st.close()
        nc.finalize()
        return nc
    with (
        nc.sbuf_tensor("p16", [128, 2, 16, WIN], BF16) as p16,
        nc.sbuf_tensor("p16n", [128, 16, WIN], BF16) as p16n,
        nc.sbuf_tensor("pt16", [128, 8, WIN], BF16) as pt16,
        nc.sbuf_tensor("ao16", [128, 2, 8, 128], BF16) as ao16,
        nc.sbuf_tensor("x32", [128, 2, D], F32) as x32,
        nc.sbuf_tensor("z32", [128, D], F32) as z32,
        nc.sbuf_tensor("sqd", [128, D], F32) as sqd,
        nc.sbuf_tensor("sall", [128, 16], F32) as sall,
        nc.sbuf_tensor("rall", [128, 2, 16], F32) as rall,
        nc.sbuf_tensor("lnt", [128, 8], F32) as lnt,
        nc.psum_tensor("psc", [128, 3, 512], F32) as psc,
        nc.psum_tensor("po", [128, 512], F32) as po,
        nc.psum_tensor("pwo", [128, 1024], F32) as pwo,
        ExitStack() as bst,
    ):
        bQK = Chan(nc, bst, "bQK")
        trs = [Chan(nc, bst, "bTR%d" % i) for i in range(4)]
        bEX = Chan(nc, bst, "bEX")
        bRC = Chan(nc, bst, "bRC")
        bSC = Chan(nc, bst, "bSC")
        bTM = Chan(nc, bst, "bTM")
        bPE2S = Chan(nc, bst, "bPE2S")
        bPE2V = Chan(nc, bst, "bPE2V")
        bAV = Chan(nc, bst, "bAV")
        bPOE = Chan(nc, bst, "bPOE")
        bWO = Chan(nc, bst, "bWO")
        bZ = Chan(nc, bst, "bZ")
        bSQ = Chan(nc, bst, "bSQ")
        bVAR = Chan(nc, bst, "bVAR")
        bRST = Chan(nc, bst, "bRST")
        bY = Chan(nc, bst, "bY")

        sch = Sched()
        pt_toks = {}
        tr_toks = {}
        for _eng in ("tensor", "vector", "scalar", "gpsimd", "sync"):
            sch.emit(_eng, lambda e: wtok(e, v_ident, v_maskq, aVT_final))
        for fw in wlate_defer:
            sch.emit("gpsimd", fw)
        sch.emit("scalar", lambda e: load_act_set(e, 6, nc))

        prev = None
        prevprev = None
        for tau in range(NTAU):
            st_ = {"qkpe": [], "exp": [], "scale": [], "tpe": [], "evac": [],
                   "avpe": [], "po": [], "tail": Sched(), "rc": None, "fx": None}
            vx32 = dX32.post(16)
            def fx(e, tau=tau, vx32=vx32):
                e.wait_ge(dX32.sem, vx32 - 16)
                if tau >= 2:
                    e.wait_ge(bZ.sem, tau - 1)
                e.dma_start(x32[:, tau % 2, :],
                            xpad.ap()[129 + 128 * tau:257 + 128 * tau, :]) \
                    .then_inc(dX32.sem, 16)
            st_["fx"] = fx

            for h in range(HEADS):
                H = 16 * tau + h
                g = h // 8
                ch = h % 8
                hb = 64 * g
                vqk = bQK.post()
                if h % 2 == 0:
                    def fmk2(e, tau=tau, H=H):
                        for dh in range(2):
                            HH = H + dh
                            if HH >= 3:
                                e.wait_ge(bEX.sem, HH - 2)
                            e.matmul(psc[:, HH % 3, 0:WIN], ident[:, :], maskq[:, :],
                                     start=True, stop=False)
                    st_["qkpe"].append(fmk2)
                qb = 64 * (h % 2)
                ksrc = k16 if ((h % 2 == 0) == (g == 0)) else k16b
                def fqk(e, tau=tau, h=h, g=g, qb=qb, ksrc=ksrc, H=H):
                    e.matmul(psc[:, H % 3, 0:WIN],
                             qy[qb:qb + 64, tau, h // 2, :],
                             ksrc[qb:qb + 64, 1 + 128 * tau:1 + 128 * tau + WIN],
                             start=False, stop=True).then_inc(bQK.sem, 1)
                st_["qkpe"].append(fqk)
                vex = bEX.post()
                def fex(e, tau=tau, h=h, H=H, vqk=vqk):
                    e.wait_ge(bQK.sem, vqk)
                    if tau >= 2:
                        e.wait_ge(bSC.sem, 16 * (tau - 2) + h + 1)
                    if tau >= 1:
                        e.wait_ge(bRC.sem, 2 * (tau - 1) + h // 8 + 1)
                    e.activation(p16[:, tau % 2, h, :], psc[:, H % 3, 0:WIN], AF.Exp,
                                 bias=konst[:, 0:1],
                                 accum_out=sall[:, h:h + 1]).then_inc(bEX.sem, 1)
                st_["exp"].append(fex)

            rcs = []
            for half in range(2):
                need_ex = 16 * tau + 8 * (half + 1)
                vrc = bRC.post()
                def frc(e, tau=tau, half=half, need_ex=need_ex):
                    e.wait_ge(bEX.sem, need_ex)
                    if tau >= 2:
                        e.wait_ge(bSC.sem, 16 * (tau - 1))
                    e.reciprocal(rall[:, tau % 2, 8 * half:8 * half + 8],
                                 sall[:, 8 * half:8 * half + 8]).then_inc(bRC.sem, 1)
                rcs.append(frc)
            st_["rc"] = rcs

            for h in range(HEADS):
                H = 16 * tau + h
                vsc = bSC.post()
                vrc_h = 2 * tau + h // 8 + 1
                war_tr = tr_toks.get((tau - 1, h // 2))
                def fsc(e, tau=tau, h=h, H=H, vrc=vrc_h, war=war_tr):
                    e.wait_ge(bRC.sem, vrc)
                    if war is not None:
                        wtok(e, war)
                    e.tensor_scalar(p16n[:, h, :], p16[:, tau % 2, h, :],
                                    rall[:, tau % 2, h:h + 1], None, ALU.mult) \
                        .then_inc(bSC.sem, 1)
                st_["scale"].append(fsc)

            for h in range(1, HEADS, 2):
                H = 16 * tau + h
                g = h // 8
                p = h // 2
                gp = 8 * tau + p
                trc = trs[gp % 4]
                slot_a = (2 * p) % 8
                slot_b = (2 * p + 1) % 8
                war_av = 16 * tau + 2 * (p - 4) + 2 if p >= 4 else \
                    (16 * (tau - 1) + 2 * (p + 4) + 2 if tau >= 1 else 0)
                vtr = trc.post(16)
                def ftr(e, tau=tau, p=p, slot_a=slot_a, slot_b=slot_b, trc=trc,
                        vtr=vtr, war_av=war_av):
                    e.wait_ge(trc.sem, vtr - 16)
                    e.wait_ge(bSC.sem, 16 * tau + 2 * p + 2)
                    if war_av > 0:
                        e.wait_ge(bAV.sem, war_av)
                    e.dma_start(
                        pt16[:, slot_a:slot_a + 2, :].rearrange(
                            "p a (c b) -> p (a c) b", b=128),
                        p16n[:, 2 * p:2 * p + 2, :].rearrange(
                            "p a (c b) -> p (a c) b", b=128),
                        transpose=True).then_inc(trc.sem, 16)
                st_["tpe"].append(ftr)
                tok_tr = trc.tok()
                tr_toks[(tau, p)] = tok_tr
                vav = bAV.post(2)
                def fav2(e, tau=tau, h=h, g=g, H=H, p=p, tok_tr=tok_tr,
                         slot_a=slot_a, slot_b=slot_b):
                    wtok(e, tok_tr)
                    if h % 8 == 1:
                        e.wait_ge(bPOE.sem, 2 * tau + (1 if h == 9 else 0))
                    slot = (h % 8) // 2
                    last = None
                    for c in range(3):
                        lhs = vt16[:, tau + c, 64 * g:64 * g + 64]
                        last = e.matmul(
                            po[0:64, slot * 128:(slot + 1) * 128], lhs,
                            pt16[:, slot_a, c * 128:(c + 1) * 128],
                            start=(c == 0), stop=(c == 2),
                            tile_position=(0, 0), skip_group_check=True)
                        last = e.matmul(
                            po[64:128, slot * 128:(slot + 1) * 128], lhs,
                            pt16[:, slot_b, c * 128:(c + 1) * 128],
                            start=(c == 0), stop=(c == 2),
                            tile_position=(0, 64), skip_group_check=True)
                    last.then_inc(bAV.sem, 2)
                st_["avpe"].append(fav2)
                if h % 8 == 7:
                    half = h // 8
                    need_av = bAV.n
                    vpoe = bPOE.post()
                    def fpo(e, tau=tau, half=half, need_av=need_av):
                        e.wait_ge(bAV.sem, need_av)
                        if tau >= 2:
                            e.wait_ge(bWO.sem, tau - 1)
                        e.tensor_copy(ao16[:, tau % 2, 4 * half:4 * half + 4, :],
                                      po[:, 0:512].rearrange("p (a b) -> p a b", a=4)) \
                            .then_inc(bPOE.sem, 1)
                    st_["po"].append((h, fpo))
            tail = st_["tail"]
            ln_s = Sched()
            st_["ln"] = ln_s
            need_poe = bPOE.n
            vwo = bWO.post()
            def fwo(e, tau=tau, need_poe=need_poe, vbo=v_bo, vwoD=v_wlate["wo"]):
                e.wait_ge(bPOE.sem, need_poe)
                wtok(e, vbo, vwoD)
                if tau >= 1:
                    e.wait_ge(bZ.sem, tau)
                last = None
                for nh in range(2):
                    for kc in range(NCHUNK):
                        e.matmul(pwo[:, nh * 512:(nh + 1) * 512],
                                 ao16[:, tau % 2, kc, :],
                                 w16o[:, kc, nh * 512:(nh + 1) * 512],
                                 start=(kc == 0), stop=False)
                    last = e.matmul(pwo[:, nh * 512:(nh + 1) * 512],
                                    ones1[:, :], bo16[:, nh * 512:(nh + 1) * 512],
                                    start=False, stop=True)
                last.then_inc(bWO.sem, 1)
            tail.emit("tensor", fwo)
            vz = bZ.post()
            def fz(e, tau=tau, vwo=vwo, vx32=vx32):
                e.wait_ge(bWO.sem, vwo)
                e.wait_ge(dX32.sem, vx32)
                if tau >= 1:
                    e.wait_ge(bY.sem, tau)
                e.scalar_tensor_tensor(z32[:, :], pwo[:, :], 1.0,
                                       x32[:, tau % 2, :], ALU.mult, ALU.add,
                                       accum_out=lnt[:, 0:1]).then_inc(bZ.sem, 1)
            ln_s.emit("vector", fz)
            vsq = bSQ.post()
            def fsq(e, vz=vz):
                e.wait_ge(bZ.sem, vz)
                e.scalar_tensor_tensor(sqd[:, :], z32[:, :], 1.0, z32[:, :],
                                       ALU.mult, ALU.mult,
                                       accum_out=lnt[:, 1:2]).then_inc(bSQ.sem, 1)
            ln_s.emit("vector", fsq)
            vvar = bVAR.post()
            def fvar(e, vsq=vsq):
                e.wait_ge(bSQ.sem, vsq)
                selfsync(e, sfV,
                         e.tensor_scalar(lnt[:, 2:3], lnt[:, 0:1], 1.0 / 1024.0,
                                         None, ALU.mult))
                selfsync(e, sfV,
                         e.tensor_tensor(lnt[:, 3:4], lnt[:, 2:3], lnt[:, 2:3],
                                         ALU.mult))
                e.tensor_scalar(lnt[:, 4:5], lnt[:, 1:2], 1.0 / 1024.0,
                                lnt[:, 3:4], ALU.mult, ALU.subtract) \
                    .then_inc(bVAR.sem, 1)
            ln_s.emit("vector", fvar)
            vrst = bRST.post()
            def frst(e, vvar=vvar):
                e.wait_ge(bVAR.sem, vvar)
                selfsync(e, sfS,
                         e.activation(lnt[:, 5:6], lnt[:, 4:5], AF.Ln,
                                      bias=konst[:, 2:3]))
                e.activation(lnt[:, 6:7], lnt[:, 5:6], AF.Exp, scale=konst[:, 3:4]) \
                    .then_inc(bRST.sem, 1)
            ln_s.emit("scalar", frst)
            vy = bY.post()
            def fy(e, tau=tau, vrst=vrst):
                e.wait_ge(bRST.sem, vrst)
                selfsync(e, sfV,
                         e.tensor_scalar(lnt[:, 7:8], lnt[:, 2:3], lnt[:, 6:7], -1.0,
                                         ALU.mult, ALU.mult))
                if tau == 0:
                    selfsync(e, sfV,
                             e.tensor_scalar(lnt[:, 4:5], lnt[:, 6:7],
                                             bmask_s[:, 0:1], None, ALU.mult))
                    selfsync(e, sfV,
                             e.tensor_scalar(lnt[:, 5:6], lnt[:, 7:8],
                                             bmask_s[:, 0:1], None, ALU.mult))
                    e.tensor_scalar(yE[:, :], z32[:, :], lnt[:, 4:5], lnt[:, 5:6],
                                    ALU.mult, ALU.add).then_inc(bY.sem, 1)
                else:
                    dst = qy[:, tau - 1, :, :].rearrange("p a b -> p (a b)")
                    e.tensor_scalar(dst, z32[:, :], lnt[:, 6:7], lnt[:, 7:8],
                                    ALU.mult, ALU.add).then_inc(bY.sem, 1)
            ln_s.emit("vector", fy)



            sch.emit("gpsimd", st_["fx"])
            pq = st_["qkpe"]

            def emit_stage2(src_st, i):
                sch.emit("vector", src_st["scale"][2 * i])
                sch.emit("vector", src_st["scale"][2 * i + 1])
                sch.emit("sync", src_st["tpe"][i])
                sch.emit("tensor", src_st["avpe"][i])
                for hh, fp in src_st["po"]:
                    if hh == 2 * i + 1:
                        sch.emit("vector", fp)

            for i in range(8):
                if i == 3 and prevprev is not None:
                    sch.extend(prevprev["ln"])
                sch.emit("tensor", pq[3 * i])
                sch.emit("tensor", pq[3 * i + 1])
                sch.emit("tensor", pq[3 * i + 2])
                sch.emit("scalar", st_["exp"][2 * i])
                sch.emit("scalar", st_["exp"][2 * i + 1])
                if prev is not None and i >= 1:
                    emit_stage2(prev, i - 1)
                if i == 4:
                    sch.emit("vector", st_["rc"][0])
            if prev is not None:
                emit_stage2(prev, 7)
            sch.emit("vector", st_["rc"][1])
            if prev is not None:
                sch.extend(prev["tail"])
            prevprev = prev
            prev = st_

        for i in range(8):
            emit_stage2(prev, i)
            if i == 3 and prevprev is not None:
                sch.extend(prevprev["ln"])
        sch.extend(prev["tail"])
        sch.extend(prev["ln"])

        if phases == 2:
            def fdbg(e, vy=bY.n):
                e.wait_ge(bY.sem, vy)
                e.dma_start(dbg.ap()[:, 0:1024], z32[:, :]).then_inc(dOut.sem, 16)
                e.dma_start(dbg.ap()[:, 1024:2048], x32[:, (NTAU - 1) % 2, :]).then_inc(dOut.sem, 16)
                e.dma_start(dbg.ap()[:, 2048:2056], lnt[:, :]).then_inc(dOut.sem, 16)
                e.dma_start(dbg.ap()[:, 2056:2072], sall[:, :]).then_inc(dOut.sem, 16)
                e.dma_start(dbg.ap()[:, 2072:2088], rall[:, :]).then_inc(dOut.sem, 16)
                e.dma_start(dbg.ap()[:, 2088:2096], konst[:, :]).then_inc(dOut.sem, 16)
                e.dma_start(dbg.ap()[:, 2096:3120],
                            ao16[:, (NTAU - 1) % 2, :, :].rearrange("p a b -> p (a b)")) \
                    .then_inc(dOut.sem, 16)
                e.wait_ge(dOut.sem, 112)
            sch.emit("gpsimd", fdbg)
        with nc.Block() as blk:
            sch.run(blk)

    if phases < 3:
        with nc.Block() as blk:
            @blk.gpsimd
            def _(e):
                for s in range(16):
                    e.wait_ge(dOut.sem, 16 * s)
                    e.dma_start(out.ap()[128 * s:128 * (s + 1), :],
                                qy[:, s, :, :].rearrange("p a b -> p (a b)")) \
                        .then_inc(dOut.sem, 16)
                e.wait_ge(dOut.sem, 256)
        st.close()
        nc.finalize()
        return nc
    with (
        nc.sbuf_tensor("yfm", [128, 2, 8, 528], BF16) as yfm,
        nc.sbuf_tensor("d16", [128, 2, 512], BF16) as d16,
        nc.sbuf_tensor("xr16", [128, 8, 512], BF16) as xr16,
        nc.sbuf_tensor("xk16", [128, 8, 512], BF16) as xk16,
        nc.sbuf_tensor("r16", [128, 4, D], BF16) as r16,
        nc.sbuf_tensor("krelu", [128, 1, 512], BF16) as krelu,
        nc.sbuf_tensor("kact16", [128, 8, 512], BF16) as kact16,
        nc.sbuf_tensor("g32", [128, D], F32) as g32,
        nc.sbuf_tensor("z32c", [128, 2, D], F32) as z32c,
        nc.sbuf_tensor("out32", [128, D], F32) as out32,
        nc.sbuf_tensor("ybt", [128, 8, 128], BF16) as ybt,
        nc.sbuf_tensor("lnt2", [128, 8], F32) as lnt2,
        nc.psum_tensor("pr", [128, 3, 1024], F32) as pr,
        nc.psum_tensor("pk", [128, 2, 512], F32) as pk,
        ExitStack() as cst,
    ):
        cT = Chan(nc, cst, "cT")
        cTeS = Chan(nc, cst, "cTeS")
        cTeV = Chan(nc, cst, "cTeV")
        ctrs = [Chan(nc, cst, "cTR%d" % i) for i in range(2)]
        cMXV = Chan(nc, cst, "cMXV")
        cSUB = Chan(nc, cst, "cSUB")
        cMXG = Chan(nc, cst, "cMXG")
        cPRS = Chan(nc, cst, "cPRS")
        cPRV = Chan(nc, cst, "cPRV")
        cWR = Chan(nc, cst, "cWR")
        cSG = Chan(nc, cst, "cSG")
        cWK = Chan(nc, cst, "cWK")
        cRL = Chan(nc, cst, "cRL")
        cKA = Chan(nc, cst, "cKA")
        cWV = Chan(nc, cst, "cWV")
        cG = Chan(nc, cst, "cG")
        cZ2 = Chan(nc, cst, "cZ2")
        cSQ2 = Chan(nc, cst, "cSQ2")
        cVAR2 = Chan(nc, cst, "cVAR2")
        cRST2 = Chan(nc, cst, "cRST2")
        cO = Chan(nc, cst, "cO")

        sch = Sched()
        npru = 0
        cte_toks = []
        pr_toks = {}
        mx_all_tok = None
        tailsC = {}

        for G in range(NFG):
            head = Sched()
            tail = Sched()
            trcc = ctrs[G % 2]
            par = G % 2
            vtr = trcc.post(80)
            def fyt(e, G=G, par=par, trcc=trcc, vtr=vtr):
                e.wait_ge(trcc.sem, vtr - 80)
                if G >= 1:
                    e.wait_ge(cTeV.sem, G)
                if G >= 2:
                    e.wait_ge(cMXV.sem, 8 * (G - 1))
                if G == 0:
                    e.dma_start(ybt[:, :, :],
                                yE.ap().rearrange("p (a b) -> p a b", b=128),
                                transpose=True).then_inc(trcc.sem, 16)
                else:
                    e.dma_start(ybt[:, :, :], qy[:, 4 * G - 1, :, :],
                                transpose=True).then_inc(trcc.sem, 16)
                for k in range(4):
                    e.dma_start(
                        yfm[:, par, :, 16 + 128 * k:144 + 128 * k],
                        qy[:, 4 * G + k, :, :],
                        transpose=True).then_inc(trcc.sem, 16)
            head.emit("sync", fyt)
            cTeV.post()
            def fb_e(e, G=G, par=par, vtr=vtr, trcc=trcc):
                e.wait_ge(trcc.sem, vtr)
                e.tensor_copy(yfm[:, par, :, 15:16], ybt[:, :, 127:128]) \
                    .then_inc(cTeV.sem, 1)
            head.emit("vector", fb_e)
            vte_all = ((trcc.sem, vtr), cTeV.tok())
            mix_s = Sched()
            for chx in range(NCHUNK):
                vsub = cSUB.post()
                def fsub(e, G=G, par=par, chx=chx, vte=vte_all, vsub=vsub):
                    wtok(e, *vte)
                    if vsub > 2:
                        e.wait_ge(cMXV.sem, vsub - 2)
                    e.tensor_sub(d16[:, chx % 2, :], yfm[:, par, chx, 16:528],
                                 yfm[:, par, chx, 15:527]).then_inc(cSUB.sem, 1)
                mix_s.emit("gpsimd", fsub)
                vmx = cMXV.post()
                def fmx(e, G=G, par=par, chx=chx, vsub=vsub, vmx=vmx):
                    if G >= 1 and chx == 0:
                        e.wait_ge(cWR.sem, 4 * G)
                        e.wait_ge(cWK.sem, 8 * G)
                    e.wait_ge(cSUB.sem, vsub)
                    e.scalar_tensor_tensor(xr16[:, chx, :], d16[:, chx % 2, :],
                                           mur_s[:, chx:chx + 1],
                                           yfm[:, par, chx, 15:527],
                                           ALU.mult, ALU.add)
                    e.scalar_tensor_tensor(xk16[:, chx, :], d16[:, chx % 2, :],
                                           muk_s[:, chx:chx + 1],
                                           yfm[:, par, chx, 15:527],
                                           ALU.mult, ALU.add).then_inc(cMXV.sem, 1)
                mix_s.emit("vector", fmx)
            vmx_all = (cMXV.tok(),)
            mx_all_tok = vmx_all

            tail.emit("scalar", lambda e: load_act_set(e, 2, nc))
            for ts in range(4):
                npru += 1
                pru = npru
                vwr = cWR.post()
                war = pr_toks.get(pru - 3)
                def fwr(e, G=G, ts=ts, vmx=vmx_all, vwrD=v_wlate["wr"], pru=pru, war=war):
                    wtok(e, *vmx)
                    wtok(e, vwrD)
                    if war is not None:
                        wtok(e, war)
                    last = None
                    for nh in range(2):
                        for kc in range(NCHUNK):
                            last = e.matmul(pr[:, pru % 3, nh * 512:(nh + 1) * 512],
                                            xr16[:, kc, ts * 128:(ts + 1) * 128],
                                            w16r[:, kc, nh * 512:(nh + 1) * 512],
                                            start=(kc == 0), stop=(kc == NCHUNK - 1))
                    last.then_inc(cWR.sem, 1)
                tail.emit("tensor", fwr)
                vsg = cSG.post()
                pr_toks[pru] = (cSG.sem, vsg)
                def fsg(e, G=G, ts=ts, vwr=vwr, pru=pru):
                    e.wait_ge(cWR.sem, vwr)
                    if G >= 1:
                        e.wait_ge(cG.sem, 4 * G)
                    e.activation(r16[:, ts, :], pr[:, pru % 3, :], AF.Sigmoid) \
                        .then_inc(cSG.sem, 1)
                tail.emit("scalar", fsg)

            for dc in range(NCHUNK):
                vwk = cWK.post()
                def fwk(e, G=G, dc=dc, vmx=vmx_all, vwkD=v_wlate["wkf"], vwk=vwk):
                    wtok(e, *vmx)
                    wtok(e, vwkD)
                    if vwk > 2:
                        e.wait_ge(cRL.sem, vwk - 2)
                    last = None
                    for kc in range(NCHUNK):
                        last = e.matmul(pk[:, vwk % 2, 0:512],
                                        w16kf[:, kc, dc * 128:(dc + 1) * 128],
                                        xk16[:, kc, :],
                                        start=(kc == 0), stop=(kc == NCHUNK - 1))
                    last.then_inc(cWK.sem, 1)
                tail.emit("tensor", fwk)
                vrl = cRL.post()
                def frl(e, vwk=vwk, vrl=vrl):
                    e.wait_ge(cWK.sem, vwk)
                    if vrl > 1:
                        e.wait_ge(cKA.sem, vrl - 1)
                    e.activation(krelu[:, 0, :], pk[:, vwk % 2, 0:512], AF.Relu) \
                        .then_inc(cRL.sem, 1)
                tail.emit("scalar", frl)
                vka = cKA.post()
                def fka(e, G=G, dc=dc, vrl=vrl):
                    e.wait_ge(cRL.sem, vrl)
                    if G >= 1 and dc == 0:
                        e.wait_ge(cWV.sem, 4 * G)
                    e.tensor_mul(kact16[:, dc, :], krelu[:, 0, :],
                                 krelu[:, 0, :]).then_inc(cKA.sem, 1)
                tail.emit("gpsimd", fka)
            vka_all = cKA.n

            tail.emit("scalar", lambda e: load_act_set(e, 6, nc))
            for ts in range(4):
                npru += 1
                pru = npru
                vwv = cWV.post()
                war = pr_toks.get(pru - 3)
                def fwv(e, G=G, ts=ts, vka=vka_all, vwvD=v_wlate["wvf"], pru=pru, war=war):
                    e.wait_ge(cKA.sem, vka)
                    wtok(e, vwvD)
                    if war is not None:
                        wtok(e, war)
                    last = None
                    for nh in range(2):
                        for kc in range(NCHUNK):
                            last = e.matmul(pr[:, pru % 3, nh * 512:(nh + 1) * 512],
                                            kact16[:, kc, ts * 128:(ts + 1) * 128],
                                            w16vf[:, kc, nh * 512:(nh + 1) * 512],
                                            start=(kc == 0), stop=(kc == NCHUNK - 1))
                    last.then_inc(cWV.sem, 1)
                tail.emit("tensor", fwv)
                need_sg = cSG.n
                vg = cG.post()
                pr_toks[pru] = (cG.sem, vg)
                def fg(e, G=G, ts=ts, vwv=vwv, need_sg=need_sg, pru=pru):
                    e.wait_ge(cWV.sem, vwv)
                    e.wait_ge(cSG.sem, need_sg)
                    if G + ts >= 1:
                        e.wait_ge(cZ2.sem, 4 * G + ts)
                    e.tensor_mul(g32[:, :], pr[:, pru % 3, :], r16[:, ts, :]) \
                        .then_inc(cG.sem, 1)
                tail.emit("vector", fg)
                vz2 = cZ2.post()
                idx = 4 * G + ts
                def fz2(e, G=G, ts=ts, vg=vg, idx=idx):
                    e.wait_ge(cG.sem, vg)
                    if idx >= 2:
                        e.wait_ge(cO.sem, idx - 1)
                    if idx >= 1:
                        e.wait_ge(cVAR2.sem, idx)
                    e.scalar_tensor_tensor(
                        z32c[:, idx % 2, :], g32[:, :], 1.0,
                        qy[:, 4 * G + ts, :, :].rearrange("p a b -> p (a b)"),
                        ALU.mult, ALU.add,
                        accum_out=lnt2[:, 0:1]).then_inc(cZ2.sem, 1)
                tail.emit("vector", fz2)
                vsq2 = cSQ2.post()
                def fsq2(e, vz2=vz2, idx=idx):
                    e.wait_ge(cZ2.sem, vz2)
                    e.activation(g32[:, :], z32c[:, idx % 2, :], AF.Square,
                                 accum_out=lnt2[:, 1:2]).then_inc(cSQ2.sem, 1)
                tail.emit("scalar", fsq2)
                vvar2 = cVAR2.post()
                def fvar2(e, vsq2=vsq2):
                    e.wait_ge(cSQ2.sem, vsq2)
                    selfsync(e, sfV,
                             e.tensor_scalar(lnt2[:, 2:3], lnt2[:, 0:1], 1.0 / 1024.0,
                                             None, ALU.mult))
                    selfsync(e, sfV,
                             e.tensor_tensor(lnt2[:, 3:4], lnt2[:, 2:3], lnt2[:, 2:3],
                                             ALU.mult))
                    e.tensor_scalar(lnt2[:, 4:5], lnt2[:, 1:2], 1.0 / 1024.0,
                                    lnt2[:, 3:4], ALU.mult, ALU.subtract) \
                        .then_inc(cVAR2.sem, 1)
                tail.emit("vector", fvar2)
                vrst2 = cRST2.post()
                def frst2(e, vvar2=vvar2):
                    e.wait_ge(cVAR2.sem, vvar2)
                    selfsync(e, sfS,
                             e.activation(lnt2[:, 5:6], lnt2[:, 4:5], AF.Ln,
                                          bias=konst[:, 2:3]))
                    e.activation(lnt2[:, 6:7], lnt2[:, 5:6], AF.Exp,
                                 scale=konst[:, 3:4]).then_inc(cRST2.sem, 1)
                tail.emit("scalar", frst2)
                vo = cO.post()
                def fo(e, idx=idx, vrst2=vrst2):
                    e.wait_ge(cRST2.sem, vrst2)
                    if idx >= 1:
                        e.wait_ge(dOut.sem, 16 * idx)
                    selfsync(e, sfV,
                             e.tensor_scalar(lnt2[:, 7:8], lnt2[:, 2:3],
                                             lnt2[:, 6:7], -1.0,
                                             ALU.mult, ALU.mult))
                    e.tensor_scalar(out32[:, :], z32c[:, idx % 2, :],
                                    lnt2[:, 6:7], lnt2[:, 7:8],
                                    ALU.mult, ALU.add).then_inc(cO.sem, 1)
                tail.emit("vector", fo)
                vdo = dOut.post(16)
                def fdo(e, idx=idx, vo=vo, vdo=vdo):
                    e.wait_ge(dOut.sem, vdo - 16)
                    e.wait_ge(cO.sem, vo)
                    e.dma_start(out.ap()[128 * idx:128 * (idx + 1), :],
                                out32[:, :]).then_inc(dOut.sem, 16)
                tail.emit("sync", fdo)


            tailsC[G] = tail
            sch.extend(head)
            if G >= 1:
                tprev = tailsC[G - 1]
                tprev.q["vector"] = mix_s.q["vector"] + tprev.q["vector"]
                tprev.q["gpsimd"] = tprev.q["gpsimd"] + mix_s.q["gpsimd"]
                sch.extend(tprev)
            else:
                sch.extend(mix_s)
        sch.extend(tailsC[NFG - 1])

        sch.emit("sync", lambda e, v=dOut.n: e.wait_ge(dOut.sem, v))

        with nc.Block() as blk:
            sch.run(blk)

    st.close()
    nc.finalize()
    return nc


_NC_CACHE = None


def _get_nc():
    global _NC_CACHE
    if _NC_CACHE is None:
        _NC_CACHE = build_nc()
    return _NC_CACHE


def _host_inputs(L0_attn_x, L0_attn_Wq, L0_attn_bq, L0_attn_Wk, L0_attn_bk,
                 L0_attn_Wv, L0_attn_bv, L0_attn_Wo, L0_attn_bo,
                 L0_ffn_W_r, L0_ffn_W_k, L0_ffn_W_v, L0_ffn_mu_r, L0_ffn_mu_k):
    x = np.ascontiguousarray(np.asarray(L0_attn_x, dtype=np.float32))
    ident = np.eye(128, dtype=np.float32)
    r = np.arange(128)[:, None]
    w = np.arange(WIN)[None, :]
    maskq = np.where((w - r >= 0) & (w - r <= 255), 0.0, -200.0).astype(np.float32)

    shared = dict(
        wq=np.ascontiguousarray(np.asarray(L0_attn_Wq, np.float32)),
        wk=np.ascontiguousarray(np.asarray(L0_attn_Wk, np.float32)),
        wv=np.ascontiguousarray(np.asarray(L0_attn_Wv, np.float32)),
        wo=np.ascontiguousarray(np.asarray(L0_attn_Wo, np.float32)),
        wr=np.ascontiguousarray(np.asarray(L0_ffn_W_r, np.float32).T),
        wkf=np.ascontiguousarray(np.asarray(L0_ffn_W_k, np.float32).T),
        wvf=np.ascontiguousarray(np.asarray(L0_ffn_W_v, np.float32).T),
        bq=np.ascontiguousarray(np.asarray(L0_attn_bq, np.float32).reshape(8, 128).T),
        bk=np.ascontiguousarray(np.asarray(L0_attn_bk, np.float32).reshape(128, 1)),
        bv=np.ascontiguousarray(np.asarray(L0_attn_bv, np.float32).reshape(128, 1)),
        bo=np.ascontiguousarray(np.asarray(L0_attn_bo, np.float32).reshape(1, D)),
        mur=np.ascontiguousarray(np.asarray(L0_ffn_mu_r, np.float32).reshape(8, 128).T),
        muk=np.ascontiguousarray(np.asarray(L0_ffn_mu_k, np.float32).reshape(8, 128).T),
        ident_h=ident,
        maskq_h=maskq,
    )

    in_maps = []
    for core in range(8):
        b = core // 2
        h = core % 2
        t0 = h * T
        lo = t0 - 257
        xp = np.zeros((XP, D), dtype=np.float32)
        s_lo = max(lo, 0)
        s_hi = min(lo + XP, S)
        xp[s_lo - lo:s_hi - lo, :] = x[b, s_lo:s_hi, :]
        kvm = np.zeros((1, XP), dtype=np.float32)
        kvm[0, s_lo - lo:s_hi - lo] = 1.0
        kvm = np.ascontiguousarray(np.broadcast_to(kvm, (128, XP)))
        bm = np.full((128, 1), 0.0 if h == 0 else 1.0, dtype=np.float32)
        m = dict(shared)
        m["xpad"] = xp
        m["kvmask_h"] = kvm
        m["bmask"] = bm
        in_maps.append(m)
    return in_maps


def kernel(**inputs):
    in_maps = _host_inputs(**inputs)
    nc = _get_nc()
    res = run_bass_kernel_spmd(nc, in_maps, core_ids=list(range(8)))
    outs = [np.asarray(res.results[c]["out"]) for c in range(8)]
    full = np.empty((B_, S, D), dtype=np.float32)
    for core in range(8):
        b = core // 2
        h = core % 2
        full[b, h * T:(h + 1) * T, :] = outs[core]
    return full


# version 30
# speedup vs baseline: 1.0277x; 1.0210x over previous
import numpy as np
import ml_dtypes
import concourse.bass as bass
import concourse.mybir as mybir
from concourse.bass_utils import run_bass_kernel_spmd
from contextlib import ExitStack

F32 = mybir.dt.float32
BF16 = mybir.dt.bfloat16
bf16 = ml_dtypes.bfloat16
AF = mybir.ActivationFunctionType
ALU = mybir.AluOpType

D = 1024
NCHUNK = 8
B_ = 4
S = 4096
T = 2048
XP = 2560
NTAU = 17
NG = 5
NFG = 4
HEADS = 16
WIN = 384


class Chan:
    def __init__(self, nc, stack, name):
        self.sem = stack.enter_context(nc.semaphore(name))
        self.n = 0

    def post(self, k=1):
        self.n += k
        return self.n

    def tok(self):
        return (self.sem, self.n)


def wtok(e, *toks):
    for s, v in toks:
        if v > 0:
            e.wait_ge(s, v)


def load_act_set(e, idx, nc):
    e.add_instruction(mybir.InstLoadActFuncSet(
        name=nc.get_next_instruction_name(),
        act_func_set_id=idx, ins=[], outs=[]))


def selfsync(e, ch, inst):
    v = ch.post()
    inst.then_inc(ch.sem, 1)
    e.wait_ge(ch.sem, v)


class Sched:
    def __init__(self):
        self.q = {"tensor": [], "vector": [], "scalar": [], "gpsimd": [], "sync": []}

    def emit(self, eng, fn):
        self.q[eng].append(fn)

    def extend(self, other):
        for k in self.q:
            self.q[k].extend(other.q[k])

    def run(self, block):
        qq = self.q
        if qq["sync"]:
            @block.sync
            def _(e):
                for fn in qq["sync"]:
                    fn(e)
        if qq["gpsimd"]:
            @block.gpsimd
            def _(e):
                for fn in qq["gpsimd"]:
                    fn(e)
        if qq["tensor"]:
            @block.tensor
            def _(e):
                for fn in qq["tensor"]:
                    fn(e)
        if qq["scalar"]:
            @block.scalar
            def _(e):
                for fn in qq["scalar"]:
                    fn(e)
        if qq["vector"]:
            @block.vector
            def _(e):
                for fn in qq["vector"]:
                    fn(e)


def build_nc(phases=3, astage=9):
    nc = bass.Bass("TRN2", target_bir_lowering=False)
    st = ExitStack()

    xpad = nc.dram_tensor("xpad", [XP, D], F32, kind="ExternalInput")
    kvmask_in = nc.dram_tensor("kvmask_h", [128, XP], F32, kind="ExternalInput")
    bmask_in = nc.dram_tensor("bmask", [128, 1], F32, kind="ExternalInput")
    wq_in = nc.dram_tensor("wq", [D, D], F32, kind="ExternalInput")
    wk_in = nc.dram_tensor("wk", [D, 128], F32, kind="ExternalInput")
    wv_in = nc.dram_tensor("wv", [D, 128], F32, kind="ExternalInput")
    wo_in = nc.dram_tensor("wo", [D, D], F32, kind="ExternalInput")
    wr_in = nc.dram_tensor("wr", [D, D], F32, kind="ExternalInput")
    wkf_in = nc.dram_tensor("wkf", [D, D], F32, kind="ExternalInput")
    wvf_in = nc.dram_tensor("wvf", [D, D], F32, kind="ExternalInput")
    bq_in = nc.dram_tensor("bq", [128, 8], F32, kind="ExternalInput")
    bk_in = nc.dram_tensor("bk", [128, 1], F32, kind="ExternalInput")
    bv_in = nc.dram_tensor("bv", [128, 1], F32, kind="ExternalInput")
    bo_in = nc.dram_tensor("bo", [1, D], F32, kind="ExternalInput")
    mur_in = nc.dram_tensor("mur", [128, 8], F32, kind="ExternalInput")
    muk_in = nc.dram_tensor("muk", [128, 8], F32, kind="ExternalInput")
    ident_in = nc.dram_tensor("ident_h", [128, 128], F32, kind="ExternalInput")
    maskq_in = nc.dram_tensor("maskq_h", [128, WIN], F32, kind="ExternalInput")
    out = nc.dram_tensor("out", [T, D], F32, kind="ExternalOutput")
    dbg = nc.dram_tensor("dbg", [128, 4096], F32, kind="ExternalOutput")

    w16q = st.enter_context(nc.sbuf_tensor("w16q", [128, 8, D], BF16))
    w16k = st.enter_context(nc.sbuf_tensor("w16k", [128, 8, 128], BF16))
    w16v = st.enter_context(nc.sbuf_tensor("w16v", [128, 8, 128], BF16))
    w16o = st.enter_context(nc.sbuf_tensor("w16o", [128, 8, D], BF16))
    w16r = st.enter_context(nc.sbuf_tensor("w16r", [128, 8, D], BF16))
    w16kf = st.enter_context(nc.sbuf_tensor("w16kf", [128, 8, D], BF16))
    w16vf = st.enter_context(nc.sbuf_tensor("w16vf", [128, 8, D], BF16))
    qy = st.enter_context(nc.sbuf_tensor("qy", [128, NTAU, 8, 128], BF16))
    k16 = st.enter_context(nc.sbuf_tensor("k16", [128, XP], BF16))
    k16b = st.enter_context(nc.sbuf_tensor("k16b", [128, XP], BF16))
    vt16 = st.enter_context(nc.sbuf_tensor("vt16", [128, 19, 128], BF16))
    yE = st.enter_context(nc.sbuf_tensor("yE", [128, D], BF16))
    ident = st.enter_context(nc.sbuf_tensor("ident", [128, 128], BF16))
    maskq = st.enter_context(nc.sbuf_tensor("maskq", [128, WIN], BF16))
    bo16 = st.enter_context(nc.sbuf_tensor("bo16", [1, D], BF16))
    ones1 = st.enter_context(nc.sbuf_tensor("ones1", [1, 128], BF16))
    bq_s = st.enter_context(nc.sbuf_tensor("bq_s", [128, 8], F32))
    bk_s = st.enter_context(nc.sbuf_tensor("bk_s", [128, 1], F32))
    bv_s = st.enter_context(nc.sbuf_tensor("bv_s", [128, 1], F32))
    mur_s = st.enter_context(nc.sbuf_tensor("mur_s", [128, 8], F32))
    muk_s = st.enter_context(nc.sbuf_tensor("muk_s", [128, 8], F32))
    bmask_s = st.enter_context(nc.sbuf_tensor("bmask_s", [128, 1], F32))
    konst = st.enter_context(nc.sbuf_tensor("konst", [128, 8], F32))

    chst = st.enter_context(ExitStack())
    sfV = Chan(nc, chst, "sfV")
    sfS = Chan(nc, chst, "sfS")
    sfG = Chan(nc, chst, "sfG")
    dW = Chan(nc, chst, "dW")
    dC = Chan(nc, chst, "dC")
    dX32 = Chan(nc, chst, "dX32")
    dOut = Chan(nc, chst, "dOut")

    with (
        nc.sbuf_tensor("x16t", [128, 2, 4, D], BF16) as x16t,
        nc.sbuf_tensor("xfm", [128, 8, 768], BF16) as xfm,
        nc.sbuf_tensor("v16", [128, XP], BF16) as v16,
        nc.sbuf_tensor("vbt", [128, 1, 1], BF16) as vbt,
        nc.sbuf_tensor("kvmask", [128, XP], BF16) as kvmask,
        nc.psum_tensor("paT", [128, 2, 512], F32) as paT,
        nc.psum_tensor("paP", [128, 2, 512], F32) as paP,
        nc.psum_tensor("paV", [128, 2, 512], F32) as paV,
        ExitStack() as ast,
    ):
        aT = Chan(nc, ast, "aT")
        aTeS = Chan(nc, ast, "aTeS")
        aTeV = Chan(nc, ast, "aTeV")
        aP = Chan(nc, ast, "aP")
        aPeS = Chan(nc, ast, "aPeS")
        aPeV = Chan(nc, ast, "aPeV")
        aVT = Chan(nc, chst, "aVT")
        aVTe = Chan(nc, ast, "aVTe")
        gd_chans = []
        atrs = [Chan(nc, ast, "aTR%d" % i) for i in range(3)]
        xfm_war = None

        sch = Sched()

        for dst, src in (
            (bq_s[:, :], bq_in.ap()),
            (bk_s[:, :], bk_in.ap()),
            (bv_s[:, :], bv_in.ap()),
            (mur_s[:, :], mur_in.ap()),
            (muk_s[:, :], muk_in.ap()),
            (bmask_s[:, :], bmask_in.ap()),
        ):
            v = dC.post(16)
            def fc(e, d=dst, s=src, v=v):
                e.wait_ge(dC.sem, v - 16)
                e.dma_start(d, s).then_inc(dC.sem, 16)
            sch.emit("sync", fc)
        n_dC = dC.n

        def g_dma(dst, src, wait_sem=None, wait_val=0):
            ch = Chan(nc, chst, "gd%d" % len(gd_chans))
            gd_chans.append(ch)
            ch.post(16)
            def f(e, d=dst, s=src, ws=wait_sem, wv=wait_val, ch=ch):
                if ws is not None and wv > 0:
                    e.wait_ge(ws, wv)
                e.dma_start(d, s).then_inc(ch.sem, 16)
            sch.emit("gpsimd", f)
            return ch.tok()

        v_ident = g_dma(ident[:, :], ident_in.ap())
        v_maskq = g_dma(maskq[:, :], maskq_in.ap())
        v_kvm = g_dma(kvmask[:, :], kvmask_in.ap())
        v_wk = g_dma(w16k[:, :, :], wk_in.ap().rearrange("(c p) n -> p c n", p=128))
        v_wv = g_dma(w16v[:, :, :], wv_in.ap().rearrange("(c p) n -> p c n", p=128))
        v_wq = g_dma(w16q[:, :, :], wq_in.ap().rearrange("(c p) n -> p c n", p=128))
        v_bo = g_dma(bo16[:, :], bo_in.ap())
        wlate = [
            ("wo", w16o, wo_in), ("wr", w16r, wr_in),
            ("wkf", w16kf, wkf_in), ("wvf", w16vf, wvf_in),
        ]
        v_wlate = {}

        sch.emit("scalar", lambda e: load_act_set(e, 6, nc))
        for col, val in ((0, -20.0), (2, 1e-5), (3, -0.5)):
            sch.emit("vector", lambda e, c=col, v=val: e.memset(konst[:, c:c + 1], v))
        sch.emit("vector", lambda e: e.memset(ones1[:, :], 1.0))

        ev_toks = []
        pe_toks = {}
        vt_done = 0
        npr = 0

        xb_tok = {}
        xb_war = {}
        def load_xbatch(b):
            if b in xb_tok or b >= 5:
                return
            rows = min(512, XP - 512 * b)
            ch = Chan(nc, chst, "gx%d" % b)
            ch.post(16)
            war = xb_war.get(b)
            def f(e, b=b, rows=rows, ch=ch, war=war):
                if war:
                    wtok(e, *war)
                e.dma_start(
                    x16t[:, b % 2, 0:rows // 128, :],
                    xpad.ap()[512 * b:512 * b + rows, :].rearrange(
                        "(a p) n -> p a n", p=128)).then_inc(ch.sem, 16)
            sch.emit("gpsimd", f)
            xb_tok[b] = ch.tok()
        for g in range(NG if astage >= 1 else 0):
            tiles = list(range(4 * g, min(4 * g + 6, 20)))
            tp_after_tile = {}
            load_xbatch(g)
            load_xbatch(g + 1)
            grp_tr = []
            for j, m in enumerate(tiles):
                if astage < 2:
                    continue
                slot = (m // 4) % 2
                jj = m % 4
                vdma = xb_tok[m // 4]
                col0 = 128 * j
                trc = atrs[aT.n % 3]
                vt = aT.post()
                vtr = trc.post(16)
                war = xfm_war if g >= 1 else None
                def ftx(e, slot=slot, jj=jj, col0=col0, vdma=vdma, trc=trc, vtr=vtr,
                        war=war):
                    e.wait_ge(trc.sem, vtr - 16)
                    wtok(e, vdma)
                    if war is not None:
                        wtok(e, *war)
                    e.dma_start(xfm[:, :, col0:col0 + 128],
                                x16t[:, slot, jj, :], transpose=True) \
                        .then_inc(trc.sem, 16)
                sch.emit("sync", ftx)
                grp_tr.append(trc.tok())
                tp_after_tile[j] = aT.n
            xb_war[g + 2] = tuple(grp_tr)
            ev_all = tuple(grp_tr)
            xfm_war = (aP.tok(),)

            for (name, wtile, vw, dst16, bias) in ((
                ("k", w16k, v_wk, k16, bk_s),
                ("v", w16v, v_wv, v16, bv_s),
            ) if astage >= 3 else ()):
                npr += 1
                par = npr % 2
                vP = aP.post()
                prev = npr - 2
                def fkv(e, par=par, vw=vw, wtile=wtile, ev=ev_all, vP=vP, prev=prev):
                    wtok(e, *ev)
                    wtok(e, vw)
                    if prev >= 1:
                        wtok(e, *pe_toks[prev])
                    last = None
                    for kc in range(NCHUNK):
                        last = e.matmul(paP[:, par, 0:512],
                                        wtile[:, kc, :], xfm[:, kc, 0:512],
                                        start=(kc == 0), stop=(kc == NCHUNK - 1))
                    last.then_inc(aP.sem, 1)
                sch.emit("tensor", fkv)
                aPeV.post()
                isk = (name == "k")
                def fkve(e, g=g, par=par, vP=vP, dst16=dst16, bias=bias, vkm=v_kvm,
                         isk=isk):
                    e.wait_ge(aP.sem, vP)
                    wtok(e, vkm)
                    e.wait_ge(dC.sem, n_dC)
                    if isk:
                        e.scalar_tensor_tensor(
                            dst16[:, 512 * g:512 * g + 512],
                            paP[:, par, 0:512], bias[:, 0:1],
                            kvmask[:, 512 * g:512 * g + 512],
                            ALU.add, ALU.mult).then_inc(aPeV.sem, 1)
                    else:
                        lo = 512 * g - 1
                        s0 = 1 if g == 0 else 0
                        e.scalar_tensor_tensor(
                            dst16[:, lo + s0:512 * g + 511],
                            paP[:, par, s0:512], bias[:, 0:1],
                            kvmask[:, 512 * g + s0:512 * g + 512],
                            ALU.add, ALU.mult).then_inc(aPeV.sem, 1)
                    if isk:
                        e.scalar_tensor_tensor(
                            k16b[64:128, 512 * g:512 * g + 512],
                            paP[0:64, par, 0:512], bias[0:64, 0:1],
                            kvmask[0:64, 512 * g:512 * g + 512],
                            ALU.add, ALU.mult)
                        e.scalar_tensor_tensor(
                            k16b[0:64, 512 * g:512 * g + 512],
                            paP[64:128, par, 0:512], bias[64:128, 0:1],
                            kvmask[64:128, 512 * g:512 * g + 512],
                            ALU.add, ALU.mult).then_inc(aPeV.sem, 1)
                sch.emit("vector", fkve)
                if isk:
                    aPeV.post()
                pe_toks[npr] = [aPeV.tok()]
            v_after_v = [aPeV.tok()]

            kmax = min((512 * (g + 1) - 129) // 128, 18) if astage >= 4 else -1
            if vt_done <= kmax:
                klo, khi = vt_done, kmax
                vv = aVT.post(16)
                def fvt(e, klo=klo, khi=khi, vafter=tuple(v_after_v), vv=vv):
                    e.wait_ge(aVT.sem, vv - 16)
                    wtok(e, *vafter)
                    e.dma_start(vt16[:, klo:khi + 1, :],
                                v16[:, 128 * klo:128 * (khi + 1)],
                                transpose=True).then_inc(aVT.sem, 16)
                sch.emit("sync", fvt)
                vt_done = kmax + 1

            ncols = 512 if g < 4 else 128
            nslots = ncols // 128
            for dc in range(NCHUNK if astage >= 5 else 0):
                npr += 1
                par = npr % 2
                vP = aP.post()
                prev = npr - 2
                def fq(e, dc=dc, par=par, ev=ev_all, vw=v_wq, vP=vP, prev=prev, ncols=ncols):
                    wtok(e, *ev)
                    wtok(e, vw)
                    if prev >= 1:
                        wtok(e, *pe_toks[prev])
                    last = None
                    for kc in range(NCHUNK):
                        last = e.matmul(paP[:, par, 0:ncols],
                                        w16q[:, kc, dc * 128:(dc + 1) * 128],
                                        xfm[:, kc, 129:129 + ncols],
                                        start=(kc == 0), stop=(kc == NCHUNK - 1))
                    last.then_inc(aP.sem, 1)
                sch.emit("tensor", fq)
                eng = "scalar" if (dc % 2 == 0) else "vector"
                ech = aPeS if eng == "scalar" else aPeV
                ech.post()
                def fqe(e, dc=dc, par=par, vP=vP, g=g, nslots=nslots, ncols=ncols,
                        eng=eng, ech=ech):
                    e.wait_ge(aP.sem, vP)
                    e.wait_ge(dC.sem, n_dC)
                    dst = qy[:, 4 * g:4 * g + nslots, dc, :]
                    src = paP[:, par, 0:ncols].rearrange("p (a b) -> p a b", a=nslots)
                    if eng == "scalar":
                        e.activation(dst, src, AF.Identity,
                                     bias=bq_s[:, dc:dc + 1]).then_inc(ech.sem, 1)
                    else:
                        e.tensor_scalar(dst, src, bq_s[:, dc:dc + 1],
                                        None, ALU.add).then_inc(ech.sem, 1)
                sch.emit(eng, fqe)
                pe_toks[npr] = [ech.tok()]
            xfm_war = (aP.tok(),)

        aVT_final = aVT.tok()
        wlate_defer = []
        while wlate:
            nm, wt, win_ = wlate.pop(0)
            ch = Chan(nc, chst, "gw_%s" % nm)
            ch.post(16)
            def fw(e, d=wt, s=win_, ch=ch):
                e.dma_start(d[:, :, :],
                            s.ap().rearrange("(c p) n -> p c n", p=128)) \
                    .then_inc(ch.sem, 16)
            wlate_defer.append(fw)
            v_wlate[nm] = ch.tok()
        with nc.Block() as blk:
            sch.run(blk)

    if phases < 2:
        with nc.Block() as blk:
            @blk.sync
            def _(e):
                e.dma_start(out.ap()[0:1, 0:8], bq_s[0:1, :]).then_inc(dOut.sem, 16)
                e.wait_ge(dOut.sem, 16)
        st.close()
        nc.finalize()
        return nc
    with (
        nc.sbuf_tensor("p16", [128, 2, 16, WIN], BF16) as p16,
        nc.sbuf_tensor("p16n", [128, 16, WIN], BF16) as p16n,
        nc.sbuf_tensor("pt16", [128, 8, WIN], BF16) as pt16,
        nc.sbuf_tensor("ao16", [128, 2, 8, 128], BF16) as ao16,
        nc.sbuf_tensor("x32", [128, 2, D], F32) as x32,
        nc.sbuf_tensor("z32", [128, D], F32) as z32,
        nc.sbuf_tensor("sqd", [128, D], F32) as sqd,
        nc.sbuf_tensor("sall", [128, 16], F32) as sall,
        nc.sbuf_tensor("rall", [128, 2, 16], F32) as rall,
        nc.sbuf_tensor("lnt", [128, 8], F32) as lnt,
        nc.psum_tensor("psc", [128, 5, 512], F32) as psc,
        nc.psum_tensor("po", [128, 512], F32) as po,
        nc.psum_tensor("pwo", [128, 1024], F32) as pwo,
        ExitStack() as bst,
    ):
        bQK = Chan(nc, bst, "bQK")
        trs = [Chan(nc, bst, "bTR%d" % i) for i in range(4)]
        bEX = Chan(nc, bst, "bEX")
        bRC = Chan(nc, bst, "bRC")
        bSC = Chan(nc, bst, "bSC")
        bTM = Chan(nc, bst, "bTM")
        bPE2S = Chan(nc, bst, "bPE2S")
        bPE2V = Chan(nc, bst, "bPE2V")
        bAV = Chan(nc, bst, "bAV")
        bPOE = Chan(nc, bst, "bPOE")
        bWO = Chan(nc, bst, "bWO")
        bZ = Chan(nc, bst, "bZ")
        bSQ = Chan(nc, bst, "bSQ")
        bVAR = Chan(nc, bst, "bVAR")
        bRST = Chan(nc, bst, "bRST")
        bY = Chan(nc, bst, "bY")

        sch = Sched()
        pt_toks = {}
        tr_toks = {}
        for _eng in ("tensor", "vector", "scalar", "gpsimd", "sync"):
            sch.emit(_eng, lambda e: wtok(e, v_ident, v_maskq, aVT_final))
        for fw in wlate_defer:
            sch.emit("gpsimd", fw)
        sch.emit("scalar", lambda e: load_act_set(e, 6, nc))

        prev = None
        prevprev = None
        for tau in range(NTAU):
            st_ = {"qkpe": [], "exp": [], "scale": [], "tpe": [], "evac": [],
                   "avpe": [], "po": [], "tail": Sched(), "rc": None, "fx": None}
            vx32 = dX32.post(16)
            def fx(e, tau=tau, vx32=vx32):
                e.wait_ge(dX32.sem, vx32 - 16)
                if tau >= 2:
                    e.wait_ge(bZ.sem, tau - 1)
                e.dma_start(x32[:, tau % 2, :],
                            xpad.ap()[129 + 128 * tau:257 + 128 * tau, :]) \
                    .then_inc(dX32.sem, 16)
            st_["fx"] = fx

            for h in range(HEADS):
                H = 16 * tau + h
                g = h // 8
                ch = h % 8
                hb = 64 * g
                vqk = bQK.post()
                if h % 2 == 0:
                    def fmk2(e, tau=tau, H=H):
                        for dh in range(2):
                            HH = H + dh
                            if HH >= 5:
                                e.wait_ge(bEX.sem, HH - 4)
                            e.matmul(psc[:, HH % 5, 0:WIN], ident[:, :], maskq[:, :],
                                     start=True, stop=False)
                    st_["qkpe"].append(fmk2)
                qb = 64 * (h % 2)
                ksrc = k16 if ((h % 2 == 0) == (g == 0)) else k16b
                def fqk(e, tau=tau, h=h, g=g, qb=qb, ksrc=ksrc, H=H):
                    e.matmul(psc[:, H % 5, 0:WIN],
                             qy[qb:qb + 64, tau, h // 2, :],
                             ksrc[qb:qb + 64, 1 + 128 * tau:1 + 128 * tau + WIN],
                             start=False, stop=True).then_inc(bQK.sem, 1)
                st_["qkpe"].append(fqk)
                vex = bEX.post()
                def fex(e, tau=tau, h=h, H=H, vqk=vqk):
                    e.wait_ge(bQK.sem, vqk)
                    if tau >= 2:
                        e.wait_ge(bSC.sem, 16 * (tau - 2) + h + 1)
                    if tau >= 1:
                        e.wait_ge(bRC.sem, 2 * (tau - 1) + h // 8 + 1)
                    e.activation(p16[:, tau % 2, h, :], psc[:, H % 5, 0:WIN], AF.Exp,
                                 bias=konst[:, 0:1],
                                 accum_out=sall[:, h:h + 1]).then_inc(bEX.sem, 1)
                st_["exp"].append(fex)

            rcs = []
            for half in range(2):
                need_ex = 16 * tau + 8 * (half + 1)
                vrc = bRC.post()
                def frc(e, tau=tau, half=half, need_ex=need_ex):
                    e.wait_ge(bEX.sem, need_ex)
                    if tau >= 2:
                        e.wait_ge(bSC.sem, 16 * (tau - 1))
                    e.reciprocal(rall[:, tau % 2, 8 * half:8 * half + 8],
                                 sall[:, 8 * half:8 * half + 8]).then_inc(bRC.sem, 1)
                rcs.append(frc)
            st_["rc"] = rcs

            for h in range(HEADS):
                H = 16 * tau + h
                vsc = bSC.post()
                vrc_h = 2 * tau + h // 8 + 1
                war_tr = tr_toks.get((tau - 1, h // 2))
                def fsc(e, tau=tau, h=h, H=H, vrc=vrc_h, war=war_tr):
                    e.wait_ge(bRC.sem, vrc)
                    if war is not None:
                        wtok(e, war)
                    e.tensor_scalar(p16n[:, h, :], p16[:, tau % 2, h, :],
                                    rall[:, tau % 2, h:h + 1], None, ALU.mult) \
                        .then_inc(bSC.sem, 1)
                st_["scale"].append(fsc)

            for h in range(1, HEADS, 2):
                H = 16 * tau + h
                g = h // 8
                p = h // 2
                gp = 8 * tau + p
                trc = trs[gp % 4]
                slot_a = (2 * p) % 8
                slot_b = (2 * p + 1) % 8
                war_av = 16 * tau + 2 * (p - 4) + 2 if p >= 4 else \
                    (16 * (tau - 1) + 2 * (p + 4) + 2 if tau >= 1 else 0)
                vtr = trc.post(16)
                def ftr(e, tau=tau, p=p, slot_a=slot_a, slot_b=slot_b, trc=trc,
                        vtr=vtr, war_av=war_av):
                    e.wait_ge(trc.sem, vtr - 16)
                    e.wait_ge(bSC.sem, 16 * tau + 2 * p + 2)
                    if war_av > 0:
                        e.wait_ge(bAV.sem, war_av)
                    e.dma_start(
                        pt16[:, slot_a:slot_a + 2, :].rearrange(
                            "p a (c b) -> p (a c) b", b=128),
                        p16n[:, 2 * p:2 * p + 2, :].rearrange(
                            "p a (c b) -> p (a c) b", b=128),
                        transpose=True).then_inc(trc.sem, 16)
                st_["tpe"].append(ftr)
                tok_tr = trc.tok()
                tr_toks[(tau, p)] = tok_tr
                vav = bAV.post(2)
                def fav2(e, tau=tau, h=h, g=g, H=H, p=p, tok_tr=tok_tr,
                         slot_a=slot_a, slot_b=slot_b):
                    wtok(e, tok_tr)
                    if h % 8 == 1:
                        e.wait_ge(bPOE.sem, 2 * tau + (1 if h == 9 else 0))
                    slot = (h % 8) // 2
                    last = None
                    for c in range(3):
                        lhs = vt16[:, tau + c, 64 * g:64 * g + 64]
                        last = e.matmul(
                            po[0:64, slot * 128:(slot + 1) * 128], lhs,
                            pt16[:, slot_a, c * 128:(c + 1) * 128],
                            start=(c == 0), stop=(c == 2),
                            tile_position=(0, 0), skip_group_check=True)
                        last = e.matmul(
                            po[64:128, slot * 128:(slot + 1) * 128], lhs,
                            pt16[:, slot_b, c * 128:(c + 1) * 128],
                            start=(c == 0), stop=(c == 2),
                            tile_position=(0, 64), skip_group_check=True)
                    last.then_inc(bAV.sem, 2)
                st_["avpe"].append(fav2)
                if h % 8 == 7:
                    half = h // 8
                    need_av = bAV.n
                    vpoe = bPOE.post()
                    def fpo(e, tau=tau, half=half, need_av=need_av):
                        e.wait_ge(bAV.sem, need_av)
                        if tau >= 2:
                            e.wait_ge(bWO.sem, tau - 1)
                        e.tensor_copy(ao16[:, tau % 2, 4 * half:4 * half + 4, :],
                                      po[:, 0:512].rearrange("p (a b) -> p a b", a=4)) \
                            .then_inc(bPOE.sem, 1)
                    st_["po"].append((h, fpo))
            tail = st_["tail"]
            ln_s = Sched()
            st_["ln"] = ln_s
            need_poe = bPOE.n
            vwo = bWO.post()
            def fwo(e, tau=tau, need_poe=need_poe, vbo=v_bo, vwoD=v_wlate["wo"]):
                e.wait_ge(bPOE.sem, need_poe)
                wtok(e, vbo, vwoD)
                if tau >= 1:
                    e.wait_ge(bZ.sem, tau)
                last = None
                for nh in range(2):
                    for kc in range(NCHUNK):
                        e.matmul(pwo[:, nh * 512:(nh + 1) * 512],
                                 ao16[:, tau % 2, kc, :],
                                 w16o[:, kc, nh * 512:(nh + 1) * 512],
                                 start=(kc == 0), stop=False)
                    last = e.matmul(pwo[:, nh * 512:(nh + 1) * 512],
                                    ones1[:, :], bo16[:, nh * 512:(nh + 1) * 512],
                                    start=False, stop=True)
                last.then_inc(bWO.sem, 1)
            tail.emit("tensor", fwo)
            vz = bZ.post()
            def fz(e, tau=tau, vwo=vwo, vx32=vx32):
                e.wait_ge(bWO.sem, vwo)
                e.wait_ge(dX32.sem, vx32)
                if tau >= 1:
                    e.wait_ge(bY.sem, tau)
                e.scalar_tensor_tensor(z32[:, :], pwo[:, :], 1.0,
                                       x32[:, tau % 2, :], ALU.mult, ALU.add,
                                       accum_out=lnt[:, 0:1]).then_inc(bZ.sem, 1)
            ln_s.emit("vector", fz)
            vsq = bSQ.post()
            def fsq(e, vz=vz):
                e.wait_ge(bZ.sem, vz)
                e.scalar_tensor_tensor(sqd[:, :], z32[:, :], 1.0, z32[:, :],
                                       ALU.mult, ALU.mult,
                                       accum_out=lnt[:, 1:2]).then_inc(bSQ.sem, 1)
            ln_s.emit("vector", fsq)
            vvar = bVAR.post()
            def fvar(e, vsq=vsq):
                e.wait_ge(bSQ.sem, vsq)
                selfsync(e, sfV,
                         e.tensor_scalar(lnt[:, 2:3], lnt[:, 0:1], 1.0 / 1024.0,
                                         None, ALU.mult))
                selfsync(e, sfV,
                         e.tensor_tensor(lnt[:, 3:4], lnt[:, 2:3], lnt[:, 2:3],
                                         ALU.mult))
                e.tensor_scalar(lnt[:, 4:5], lnt[:, 1:2], 1.0 / 1024.0,
                                lnt[:, 3:4], ALU.mult, ALU.subtract) \
                    .then_inc(bVAR.sem, 1)
            ln_s.emit("vector", fvar)
            vrst = bRST.post()
            def frst(e, vvar=vvar):
                e.wait_ge(bVAR.sem, vvar)
                selfsync(e, sfS,
                         e.activation(lnt[:, 5:6], lnt[:, 4:5], AF.Ln,
                                      bias=konst[:, 2:3]))
                e.activation(lnt[:, 6:7], lnt[:, 5:6], AF.Exp, scale=konst[:, 3:4]) \
                    .then_inc(bRST.sem, 1)
            ln_s.emit("scalar", frst)
            vy = bY.post()
            def fy(e, tau=tau, vrst=vrst):
                e.wait_ge(bRST.sem, vrst)
                selfsync(e, sfV,
                         e.tensor_scalar(lnt[:, 7:8], lnt[:, 2:3], lnt[:, 6:7], -1.0,
                                         ALU.mult, ALU.mult))
                if tau == 0:
                    selfsync(e, sfV,
                             e.tensor_scalar(lnt[:, 4:5], lnt[:, 6:7],
                                             bmask_s[:, 0:1], None, ALU.mult))
                    selfsync(e, sfV,
                             e.tensor_scalar(lnt[:, 5:6], lnt[:, 7:8],
                                             bmask_s[:, 0:1], None, ALU.mult))
                    e.tensor_scalar(yE[:, :], z32[:, :], lnt[:, 4:5], lnt[:, 5:6],
                                    ALU.mult, ALU.add).then_inc(bY.sem, 1)
                else:
                    dst = qy[:, tau - 1, :, :].rearrange("p a b -> p (a b)")
                    e.tensor_scalar(dst, z32[:, :], lnt[:, 6:7], lnt[:, 7:8],
                                    ALU.mult, ALU.add).then_inc(bY.sem, 1)
            ln_s.emit("vector", fy)



            sch.emit("gpsimd", st_["fx"])
            pq = st_["qkpe"]

            def emit_stage2(src_st, i):
                sch.emit("vector", src_st["scale"][2 * i])
                sch.emit("vector", src_st["scale"][2 * i + 1])
                sch.emit("sync", src_st["tpe"][i])
                sch.emit("tensor", src_st["avpe"][i])
                for hh, fp in src_st["po"]:
                    if hh == 2 * i + 1:
                        sch.emit("vector", fp)

            for i in range(8):
                if i == 3 and prevprev is not None:
                    sch.extend(prevprev["ln"])
                sch.emit("tensor", pq[3 * i])
                sch.emit("tensor", pq[3 * i + 1])
                sch.emit("tensor", pq[3 * i + 2])
                sch.emit("scalar", st_["exp"][2 * i])
                sch.emit("scalar", st_["exp"][2 * i + 1])
                if prev is not None and i >= 1:
                    emit_stage2(prev, i - 1)
                if i == 4:
                    sch.emit("vector", st_["rc"][0])
            if prev is not None:
                emit_stage2(prev, 7)
            sch.emit("vector", st_["rc"][1])
            if prev is not None:
                sch.extend(prev["tail"])
            prevprev = prev
            prev = st_

        for i in range(8):
            emit_stage2(prev, i)
            if i == 3 and prevprev is not None:
                sch.extend(prevprev["ln"])
        sch.extend(prev["tail"])
        sch.extend(prev["ln"])

        if phases == 2:
            def fdbg(e, vy=bY.n):
                e.wait_ge(bY.sem, vy)
                e.dma_start(dbg.ap()[:, 0:1024], z32[:, :]).then_inc(dOut.sem, 16)
                e.dma_start(dbg.ap()[:, 1024:2048], x32[:, (NTAU - 1) % 2, :]).then_inc(dOut.sem, 16)
                e.dma_start(dbg.ap()[:, 2048:2056], lnt[:, :]).then_inc(dOut.sem, 16)
                e.dma_start(dbg.ap()[:, 2056:2072], sall[:, :]).then_inc(dOut.sem, 16)
                e.dma_start(dbg.ap()[:, 2072:2088], rall[:, :]).then_inc(dOut.sem, 16)
                e.dma_start(dbg.ap()[:, 2088:2096], konst[:, :]).then_inc(dOut.sem, 16)
                e.dma_start(dbg.ap()[:, 2096:3120],
                            ao16[:, (NTAU - 1) % 2, :, :].rearrange("p a b -> p (a b)")) \
                    .then_inc(dOut.sem, 16)
                e.wait_ge(dOut.sem, 112)
            sch.emit("gpsimd", fdbg)
        with nc.Block() as blk:
            sch.run(blk)

    if phases < 3:
        with nc.Block() as blk:
            @blk.gpsimd
            def _(e):
                for s in range(16):
                    e.wait_ge(dOut.sem, 16 * s)
                    e.dma_start(out.ap()[128 * s:128 * (s + 1), :],
                                qy[:, s, :, :].rearrange("p a b -> p (a b)")) \
                        .then_inc(dOut.sem, 16)
                e.wait_ge(dOut.sem, 256)
        st.close()
        nc.finalize()
        return nc
    with (
        nc.sbuf_tensor("yfm", [128, 2, 8, 528], BF16) as yfm,
        nc.sbuf_tensor("d16", [128, 2, 512], BF16) as d16,
        nc.sbuf_tensor("xr16", [128, 8, 512], BF16) as xr16,
        nc.sbuf_tensor("xk16", [128, 8, 512], BF16) as xk16,
        nc.sbuf_tensor("r16", [128, 4, D], BF16) as r16,
        nc.sbuf_tensor("krelu", [128, 1, 512], BF16) as krelu,
        nc.sbuf_tensor("kact16", [128, 8, 512], BF16) as kact16,
        nc.sbuf_tensor("g32", [128, D], F32) as g32,
        nc.sbuf_tensor("z32c", [128, 2, D], F32) as z32c,
        nc.sbuf_tensor("out32", [128, D], F32) as out32,
        nc.sbuf_tensor("ybt", [128, 8, 128], BF16) as ybt,
        nc.sbuf_tensor("lnt2", [128, 8], F32) as lnt2,
        nc.psum_tensor("pr", [128, 3, 1024], F32) as pr,
        nc.psum_tensor("pk", [128, 2, 512], F32) as pk,
        ExitStack() as cst,
    ):
        cT = Chan(nc, cst, "cT")
        cTeS = Chan(nc, cst, "cTeS")
        cTeV = Chan(nc, cst, "cTeV")
        ctrs = [Chan(nc, cst, "cTR%d" % i) for i in range(2)]
        cMXV = Chan(nc, cst, "cMXV")
        cSUB = Chan(nc, cst, "cSUB")
        cMXG = Chan(nc, cst, "cMXG")
        cPRS = Chan(nc, cst, "cPRS")
        cPRV = Chan(nc, cst, "cPRV")
        cWR = Chan(nc, cst, "cWR")
        cSG = Chan(nc, cst, "cSG")
        cWK = Chan(nc, cst, "cWK")
        cRL = Chan(nc, cst, "cRL")
        cKA = Chan(nc, cst, "cKA")
        cWV = Chan(nc, cst, "cWV")
        cG = Chan(nc, cst, "cG")
        cZ2 = Chan(nc, cst, "cZ2")
        cSQ2 = Chan(nc, cst, "cSQ2")
        cVAR2 = Chan(nc, cst, "cVAR2")
        cRST2 = Chan(nc, cst, "cRST2")
        cO = Chan(nc, cst, "cO")

        sch = Sched()
        npru = 0
        cte_toks = []
        pr_toks = {}
        mx_all_tok = None
        tailsC = {}

        for G in range(NFG):
            head = Sched()
            tail = Sched()
            trcc = ctrs[G % 2]
            par = G % 2
            vtr = trcc.post(80)
            def fyt(e, G=G, par=par, trcc=trcc, vtr=vtr):
                e.wait_ge(trcc.sem, vtr - 80)
                if G >= 1:
                    e.wait_ge(cTeV.sem, G)
                if G >= 2:
                    e.wait_ge(cMXV.sem, 8 * (G - 1))
                if G == 0:
                    e.dma_start(ybt[:, :, :],
                                yE.ap().rearrange("p (a b) -> p a b", b=128),
                                transpose=True).then_inc(trcc.sem, 16)
                else:
                    e.dma_start(ybt[:, :, :], qy[:, 4 * G - 1, :, :],
                                transpose=True).then_inc(trcc.sem, 16)
                for k in range(4):
                    e.dma_start(
                        yfm[:, par, :, 16 + 128 * k:144 + 128 * k],
                        qy[:, 4 * G + k, :, :],
                        transpose=True).then_inc(trcc.sem, 16)
            head.emit("sync", fyt)
            cTeV.post()
            def fb_e(e, G=G, par=par, vtr=vtr, trcc=trcc):
                e.wait_ge(trcc.sem, vtr)
                e.tensor_copy(yfm[:, par, :, 15:16], ybt[:, :, 127:128]) \
                    .then_inc(cTeV.sem, 1)
            head.emit("vector", fb_e)
            vte_all = ((trcc.sem, vtr), cTeV.tok())
            mix_s = Sched()
            for chx in range(NCHUNK):
                vsub = cSUB.post()
                def fsub(e, G=G, par=par, chx=chx, vte=vte_all, vsub=vsub):
                    wtok(e, *vte)
                    if vsub > 2:
                        e.wait_ge(cMXV.sem, vsub - 2)
                    e.tensor_sub(d16[:, chx % 2, :], yfm[:, par, chx, 16:528],
                                 yfm[:, par, chx, 15:527]).then_inc(cSUB.sem, 1)
                mix_s.emit("gpsimd", fsub)
                vmx = cMXV.post()
                def fmx(e, G=G, par=par, chx=chx, vsub=vsub, vmx=vmx):
                    if G >= 1 and chx == 0:
                        e.wait_ge(cWR.sem, 4 * G)
                        e.wait_ge(cWK.sem, 8 * G)
                    e.wait_ge(cSUB.sem, vsub)
                    e.scalar_tensor_tensor(xr16[:, chx, :], d16[:, chx % 2, :],
                                           mur_s[:, chx:chx + 1],
                                           yfm[:, par, chx, 15:527],
                                           ALU.mult, ALU.add)
                    e.scalar_tensor_tensor(xk16[:, chx, :], d16[:, chx % 2, :],
                                           muk_s[:, chx:chx + 1],
                                           yfm[:, par, chx, 15:527],
                                           ALU.mult, ALU.add).then_inc(cMXV.sem, 1)
                mix_s.emit("vector", fmx)
            vmx_all = (cMXV.tok(),)
            mx_all_tok = vmx_all

            tail.emit("scalar", lambda e: load_act_set(e, 2, nc))
            for ts in range(4):
                npru += 1
                pru = npru
                vwr = cWR.post()
                war = pr_toks.get(pru - 3)
                def fwr(e, G=G, ts=ts, vmx=vmx_all, vwrD=v_wlate["wr"], pru=pru, war=war):
                    wtok(e, *vmx)
                    wtok(e, vwrD)
                    if war is not None:
                        wtok(e, war)
                    last = None
                    for nh in range(2):
                        for kc in range(NCHUNK):
                            last = e.matmul(pr[:, pru % 3, nh * 512:(nh + 1) * 512],
                                            xr16[:, kc, ts * 128:(ts + 1) * 128],
                                            w16r[:, kc, nh * 512:(nh + 1) * 512],
                                            start=(kc == 0), stop=(kc == NCHUNK - 1))
                    last.then_inc(cWR.sem, 1)
                tail.emit("tensor", fwr)
                vsg = cSG.post()
                pr_toks[pru] = (cSG.sem, vsg)
                def fsg(e, G=G, ts=ts, vwr=vwr, pru=pru):
                    e.wait_ge(cWR.sem, vwr)
                    if G >= 1:
                        e.wait_ge(cG.sem, 4 * G)
                    e.activation(r16[:, ts, :], pr[:, pru % 3, :], AF.Sigmoid) \
                        .then_inc(cSG.sem, 1)
                tail.emit("scalar", fsg)

            for dc in range(NCHUNK):
                vwk = cWK.post()
                def fwk(e, G=G, dc=dc, vmx=vmx_all, vwkD=v_wlate["wkf"], vwk=vwk):
                    wtok(e, *vmx)
                    wtok(e, vwkD)
                    if vwk > 2:
                        e.wait_ge(cRL.sem, vwk - 2)
                    last = None
                    for kc in range(NCHUNK):
                        last = e.matmul(pk[:, vwk % 2, 0:512],
                                        w16kf[:, kc, dc * 128:(dc + 1) * 128],
                                        xk16[:, kc, :],
                                        start=(kc == 0), stop=(kc == NCHUNK - 1))
                    last.then_inc(cWK.sem, 1)
                tail.emit("tensor", fwk)
                vrl = cRL.post()
                def frl(e, vwk=vwk, vrl=vrl):
                    e.wait_ge(cWK.sem, vwk)
                    if vrl > 1:
                        e.wait_ge(cKA.sem, vrl - 1)
                    e.activation(krelu[:, 0, :], pk[:, vwk % 2, 0:512], AF.Relu) \
                        .then_inc(cRL.sem, 1)
                tail.emit("scalar", frl)
                vka = cKA.post()
                def fka(e, G=G, dc=dc, vrl=vrl):
                    e.wait_ge(cRL.sem, vrl)
                    if G >= 1 and dc == 0:
                        e.wait_ge(cWV.sem, 4 * G)
                    e.tensor_mul(kact16[:, dc, :], krelu[:, 0, :],
                                 krelu[:, 0, :]).then_inc(cKA.sem, 1)
                tail.emit("gpsimd", fka)
            vka_all = cKA.n

            tail.emit("scalar", lambda e: load_act_set(e, 6, nc))
            for ts in range(4):
                npru += 1
                pru = npru
                vwv = cWV.post()
                war = pr_toks.get(pru - 3)
                def fwv(e, G=G, ts=ts, vka=vka_all, vwvD=v_wlate["wvf"], pru=pru, war=war):
                    e.wait_ge(cKA.sem, vka)
                    wtok(e, vwvD)
                    if war is not None:
                        wtok(e, war)
                    last = None
                    for nh in range(2):
                        for kc in range(NCHUNK):
                            last = e.matmul(pr[:, pru % 3, nh * 512:(nh + 1) * 512],
                                            kact16[:, kc, ts * 128:(ts + 1) * 128],
                                            w16vf[:, kc, nh * 512:(nh + 1) * 512],
                                            start=(kc == 0), stop=(kc == NCHUNK - 1))
                    last.then_inc(cWV.sem, 1)
                tail.emit("tensor", fwv)
                need_sg = cSG.n
                vg = cG.post()
                pr_toks[pru] = (cG.sem, vg)
                def fg(e, G=G, ts=ts, vwv=vwv, need_sg=need_sg, pru=pru):
                    e.wait_ge(cWV.sem, vwv)
                    e.wait_ge(cSG.sem, need_sg)
                    if G + ts >= 1:
                        e.wait_ge(cZ2.sem, 4 * G + ts)
                    e.tensor_mul(g32[:, :], pr[:, pru % 3, :], r16[:, ts, :]) \
                        .then_inc(cG.sem, 1)
                tail.emit("vector", fg)
                vz2 = cZ2.post()
                idx = 4 * G + ts
                def fz2(e, G=G, ts=ts, vg=vg, idx=idx):
                    e.wait_ge(cG.sem, vg)
                    if idx >= 2:
                        e.wait_ge(cO.sem, idx - 1)
                    if idx >= 1:
                        e.wait_ge(cVAR2.sem, idx)
                    e.scalar_tensor_tensor(
                        z32c[:, idx % 2, :], g32[:, :], 1.0,
                        qy[:, 4 * G + ts, :, :].rearrange("p a b -> p (a b)"),
                        ALU.mult, ALU.add,
                        accum_out=lnt2[:, 0:1]).then_inc(cZ2.sem, 1)
                tail.emit("vector", fz2)
                vsq2 = cSQ2.post()
                def fsq2(e, vz2=vz2, idx=idx):
                    e.wait_ge(cZ2.sem, vz2)
                    e.activation(g32[:, :], z32c[:, idx % 2, :], AF.Square,
                                 accum_out=lnt2[:, 1:2]).then_inc(cSQ2.sem, 1)
                tail.emit("scalar", fsq2)
                vvar2 = cVAR2.post()
                def fvar2(e, vsq2=vsq2):
                    e.wait_ge(cSQ2.sem, vsq2)
                    selfsync(e, sfV,
                             e.tensor_scalar(lnt2[:, 2:3], lnt2[:, 0:1], 1.0 / 1024.0,
                                             None, ALU.mult))
                    selfsync(e, sfV,
                             e.tensor_tensor(lnt2[:, 3:4], lnt2[:, 2:3], lnt2[:, 2:3],
                                             ALU.mult))
                    e.tensor_scalar(lnt2[:, 4:5], lnt2[:, 1:2], 1.0 / 1024.0,
                                    lnt2[:, 3:4], ALU.mult, ALU.subtract) \
                        .then_inc(cVAR2.sem, 1)
                tail.emit("vector", fvar2)
                vrst2 = cRST2.post()
                def frst2(e, vvar2=vvar2):
                    e.wait_ge(cVAR2.sem, vvar2)
                    selfsync(e, sfS,
                             e.activation(lnt2[:, 5:6], lnt2[:, 4:5], AF.Ln,
                                          bias=konst[:, 2:3]))
                    e.activation(lnt2[:, 6:7], lnt2[:, 5:6], AF.Exp,
                                 scale=konst[:, 3:4]).then_inc(cRST2.sem, 1)
                tail.emit("scalar", frst2)
                vo = cO.post()
                def fo(e, idx=idx, vrst2=vrst2):
                    e.wait_ge(cRST2.sem, vrst2)
                    if idx >= 1:
                        e.wait_ge(dOut.sem, 16 * idx)
                    selfsync(e, sfV,
                             e.tensor_scalar(lnt2[:, 7:8], lnt2[:, 2:3],
                                             lnt2[:, 6:7], -1.0,
                                             ALU.mult, ALU.mult))
                    e.tensor_scalar(out32[:, :], z32c[:, idx % 2, :],
                                    lnt2[:, 6:7], lnt2[:, 7:8],
                                    ALU.mult, ALU.add).then_inc(cO.sem, 1)
                tail.emit("vector", fo)
                vdo = dOut.post(16)
                def fdo(e, idx=idx, vo=vo, vdo=vdo):
                    e.wait_ge(dOut.sem, vdo - 16)
                    e.wait_ge(cO.sem, vo)
                    e.dma_start(out.ap()[128 * idx:128 * (idx + 1), :],
                                out32[:, :]).then_inc(dOut.sem, 16)
                tail.emit("sync", fdo)


            tailsC[G] = tail
            sch.extend(head)
            if G >= 1:
                tprev = tailsC[G - 1]
                tprev.q["vector"] = mix_s.q["vector"] + tprev.q["vector"]
                tprev.q["gpsimd"] = tprev.q["gpsimd"] + mix_s.q["gpsimd"]
                sch.extend(tprev)
            else:
                sch.extend(mix_s)
        sch.extend(tailsC[NFG - 1])

        sch.emit("sync", lambda e, v=dOut.n: e.wait_ge(dOut.sem, v))

        with nc.Block() as blk:
            sch.run(blk)

    st.close()
    nc.finalize()
    return nc


_NC_CACHE = None


def _get_nc():
    global _NC_CACHE
    if _NC_CACHE is None:
        _NC_CACHE = build_nc()
    return _NC_CACHE


def _host_inputs(L0_attn_x, L0_attn_Wq, L0_attn_bq, L0_attn_Wk, L0_attn_bk,
                 L0_attn_Wv, L0_attn_bv, L0_attn_Wo, L0_attn_bo,
                 L0_ffn_W_r, L0_ffn_W_k, L0_ffn_W_v, L0_ffn_mu_r, L0_ffn_mu_k):
    x = np.ascontiguousarray(np.asarray(L0_attn_x, dtype=np.float32))
    ident = np.eye(128, dtype=np.float32)
    r = np.arange(128)[:, None]
    w = np.arange(WIN)[None, :]
    maskq = np.where((w - r >= 0) & (w - r <= 255), 0.0, -200.0).astype(np.float32)

    shared = dict(
        wq=np.ascontiguousarray(np.asarray(L0_attn_Wq, np.float32)),
        wk=np.ascontiguousarray(np.asarray(L0_attn_Wk, np.float32)),
        wv=np.ascontiguousarray(np.asarray(L0_attn_Wv, np.float32)),
        wo=np.ascontiguousarray(np.asarray(L0_attn_Wo, np.float32)),
        wr=np.ascontiguousarray(np.asarray(L0_ffn_W_r, np.float32).T),
        wkf=np.ascontiguousarray(np.asarray(L0_ffn_W_k, np.float32).T),
        wvf=np.ascontiguousarray(np.asarray(L0_ffn_W_v, np.float32).T),
        bq=np.ascontiguousarray(np.asarray(L0_attn_bq, np.float32).reshape(8, 128).T),
        bk=np.ascontiguousarray(np.asarray(L0_attn_bk, np.float32).reshape(128, 1)),
        bv=np.ascontiguousarray(np.asarray(L0_attn_bv, np.float32).reshape(128, 1)),
        bo=np.ascontiguousarray(np.asarray(L0_attn_bo, np.float32).reshape(1, D)),
        mur=np.ascontiguousarray(np.asarray(L0_ffn_mu_r, np.float32).reshape(8, 128).T),
        muk=np.ascontiguousarray(np.asarray(L0_ffn_mu_k, np.float32).reshape(8, 128).T),
        ident_h=ident,
        maskq_h=maskq,
    )

    in_maps = []
    for core in range(8):
        b = core // 2
        h = core % 2
        t0 = h * T
        lo = t0 - 257
        xp = np.zeros((XP, D), dtype=np.float32)
        s_lo = max(lo, 0)
        s_hi = min(lo + XP, S)
        xp[s_lo - lo:s_hi - lo, :] = x[b, s_lo:s_hi, :]
        kvm = np.zeros((1, XP), dtype=np.float32)
        kvm[0, s_lo - lo:s_hi - lo] = 1.0
        kvm = np.ascontiguousarray(np.broadcast_to(kvm, (128, XP)))
        bm = np.full((128, 1), 0.0 if h == 0 else 1.0, dtype=np.float32)
        m = dict(shared)
        m["xpad"] = xp
        m["kvmask_h"] = kvm
        m["bmask"] = bm
        in_maps.append(m)
    return in_maps


def kernel(**inputs):
    in_maps = _host_inputs(**inputs)
    nc = _get_nc()
    res = run_bass_kernel_spmd(nc, in_maps, core_ids=list(range(8)))
    outs = [np.asarray(res.results[c]["out"]) for c in range(8)]
    full = np.empty((B_, S, D), dtype=np.float32)
    for core in range(8):
        b = core // 2
        h = core % 2
        full[b, h * T:(h + 1) * T, :] = outs[core]
    return full
